# version 37
# baseline (speedup 1.0000x reference)
import contextlib
import math

import numpy as np

import concourse.bass as bass
import concourse.mybir as mybir
from concourse.bass_utils import run_bass_kernel_spmd

F32 = mybir.dt.float32
BF16 = mybir.dt.bfloat16
I32 = mybir.dt.int32
AF = mybir.ActivationFunctionType
ALU = mybir.AluOpType
AX = mybir.AxisListType

S = 2048
D = 1024
NH = 16
HD = 64
HID = 3072
NT = S // 128
NKC = D // 128
NHC = HID // 128
QB = 512
NQB = S // QB
EPS = 1e-5
QSCALE = 1.0 / 127.0

SLOPES = [2.0 ** (-0.5 * (h + 1)) for h in range(NH)]
WCH = {0: 64, 1: 128, 2: 128, 3: 256, 4: 256}
BAND_LOG = 38.0


def _chunks(h):
    w = WCH.get(h, 512)
    B = BAND_LOG / SLOPES[h]
    out = []
    for c0 in range(0, S, w):
        R = c0 + w - 1
        ki_lo = max(0, int(math.ceil((c0 - B - 127) / 128.0)))
        ki_hi = R // 128
        out.append((c0, w, R, ki_lo, ki_hi))
    return out


CHUNKS = [_chunks(h) for h in range(NH)]


class Prog:
    ENGS = ("sync", "tensor", "scalar", "vector", "gpsimd")

    def __init__(self, nc, ctx):
        self.nc = nc
        self.q = {e: [] for e in self.ENGS}
        self.sem = {}
        self.cnt = {}
        self.owner = {}
        self.wm = {e: {} for e in self.ENGS}
        self.ctx = ctx

    def add_sem(self, name, owner=None):
        self.sem[name] = self.ctx.enter_context(self.nc.semaphore(name))
        self.cnt[name] = 0
        self.owner[name] = owner

    def emit(self, eng, fn, waits=(), inc=None, inc_by=1):
        wlist = {}
        for tk in waits:
            if tk is None:
                continue
            s, v = tk
            if v <= 0 or self.owner.get(s) == eng:
                continue
            if self.wm[eng].get(s, 0) >= v:
                continue
            wlist[s] = max(wlist.get(s, 0), v)
        for s, v in wlist.items():
            self.wm[eng][s] = v
        chain = None
        if (inc is not None and eng != "tensor"
                and self.owner.get(inc) == eng and self.cnt[inc] > 0):
            chain = (inc, self.cnt[inc])
        tok = None
        if inc is not None:
            self.cnt[inc] += inc_by
            tok = (inc, self.cnt[inc])
        sems = self.sem

        def op(e, _fn=fn, _w=wlist, _inc=inc, _by=inc_by, _ch=chain):
            for s, v in _w.items():
                e.wait_ge(sems[s], v)
            ins = _fn(e)
            if _ch is not None:
                ins._wait_ge(sems[_ch[0]], _ch[1])
            if _inc is not None:
                ins.then_inc(sems[_inc], _by)

        self.q[eng].append(op)
        return tok

    def final_wait(self, eng, tok):
        s, v = tok
        sems = self.sem
        self.q[eng].append(lambda e: e.wait_ge(sems[s], v))




def _finish(nc, p):
    with nc.Block() as block:
        @block.sync
        def _(e):
            for f in p.q["sync"]:
                f(e)

        @block.tensor
        def _(e):
            for f in p.q["tensor"]:
                f(e)

        @block.scalar
        def _(e):
            for f in p.q["scalar"]:
                f(e)

        @block.vector
        def _(e):
            for f in p.q["vector"]:
                f(e)

        @block.gpsimd
        def _(e):
            for f in p.q["gpsimd"]:
                f(e)
    return nc

def build_program(stop=9):
    nc = bass.Bass("TRN2", target_bir_lowering=False)

    x_h = nc.dram_tensor("x", [S, D], F32, kind="ExternalInput").ap()
    wqT_h = nc.dram_tensor("wqT", [D, D], F32, kind="ExternalInput").ap()
    wkT_h = nc.dram_tensor("wkT", [D, D], F32, kind="ExternalInput").ap()
    wvT_h = nc.dram_tensor("wvT", [D, D], F32, kind="ExternalInput").ap()
    woT_h = nc.dram_tensor("woT", [D, D], F32, kind="ExternalInput").ap()
    qb2_h = nc.dram_tensor("qb2", [128, NKC], F32, kind="ExternalInput").ap()
    kb2_h = nc.dram_tensor("kb2", [128, NKC], F32, kind="ExternalInput").ap()
    vbr_h = nc.dram_tensor("vbr", [1, D], F32, kind="ExternalInput").ap()
    obr_h = nc.dram_tensor("obr", [1, D], F32, kind="ExternalInput").ap()
    wgT_h = nc.dram_tensor("wgT", [D, HID], I32, kind="ExternalInput").ap()
    wuT_h = nc.dram_tensor("wuT", [D, HID], I32, kind="ExternalInput").ap()
    wdT_h = nc.dram_tensor("wdT", [HID, D], I32, kind="ExternalInput").ap()
    ident_h = nc.dram_tensor("ident", [128, 128], F32, kind="ExternalInput").ap()
    tri_h = nc.dram_tensor("tri", [128, 128], F32, kind="ExternalInput").ap()
    out_h = nc.dram_tensor("out", [S, D], F32, kind="ExternalOutput").ap()

    with contextlib.ExitStack() as ctx:
        def sb(name, shape, dt):
            return ctx.enter_context(nc.sbuf_tensor(name, shape, dt))

        def pst(name, shape):
            return ctx.enter_context(nc.psum_tensor(name, shape, F32))

        xn_fm = sb("xn_fm", [128, NKC, S], BF16)
        ident_bf = sb("ident_bf", [128, 128], BF16)
        tri_bf = sb("tri_bf", [128, 128], BF16)
        qb_s = sb("qb_s", [128, NKC], F32)
        kb_s = sb("kb_s", [128, NKC], F32)
        obr_bf = sb("obr_bf", [1, D], BF16)
        ones_bf = sb("ones_bf", [1, 128], BF16)
        pvec = sb("pvec", [128, 1], F32)
        evec = sb("evec", [128, NH], F32)
        stats = sb("stats", [128, 5, 16], F32)
        xn_tm = sb("xn_tm", [128, 2, D], BF16)
        zrec = sb("zrec", [128, 16], F32)
        vs_s = sb("vs_s", [128, 12, 65], BF16)
        exp_s = sb("exp_s", [128, 2, 1024], BF16)
        trineg = sb("trineg", [128, 128], BF16)
        sq_s = sb("sq_s", [128, 1024], BF16)
        wqk = sb("wqk", [128, 2, 2, NKC, 128], BF16)

        pA = pst("pA", [128, 1024])
        pB = pst("pB", [128, 1024])
        pC = pst("pC", [128, 1024])
        pD = pst("pD", [128, 1024])
        ps_tr = pD[:, 0:512]

        p = Prog(nc, ctx)
        p.add_sem("sx")
        p.add_sem("sx2")
        p.add_sem("sk")
        p.add_sem("skg")
        p.add_sem("swa")
        p.add_sem("swb")
        p.add_sem("sfa")
        p.add_sem("sfb")
        p.add_sem("sd")
        p.add_sem("sx3")
        p.add_sem("swo")
        p.add_sem("st", "tensor")
        p.add_sem("sc", "scalar")
        p.add_sem("sv", "vector")
        p.add_sem("sg", "gpsimd")
        p.add_sem("so")
        p.add_sem("sxl")

        xq = x_h.rearrange("(tt p) d -> p tt d", p=128)
        p.emit("sync", lambda e: e.dma_start(out=qb_s[:], in_=qb2_h),
               inc="sk", inc_by=16)
        p.emit("sync", lambda e: e.dma_start(out=kb_s[:], in_=kb2_h),
               inc="sk", inc_by=16)
        p.emit("gpsimd", lambda e: e.dma_start(out=ident_bf[:], in_=ident_h),
               inc="skg", inc_by=16)
        p.emit("gpsimd", lambda e: e.dma_start(out=tri_bf[:], in_=tri_h),
               inc="skg", inc_by=16)
        constg_tok = p.emit("gpsimd", lambda e: e.dma_start(out=obr_bf[:], in_=obr_h),
                            inc="skg", inc_by=16)
        qb_tok = kb_tok = ("sk", 32)
        ident_tok = tri_tok = obr_tok = ("skg", 48)
        ones_tok = p.emit("vector", lambda e: e.memset(ones_bf[:], 1.0), inc="sv")
        trineg_tok = p.emit(
            "gpsimd",
            lambda e: e.tensor_scalar(trineg[:], tri_bf[:], -1.0, 1.0e5,
                                      ALU.add, ALU.mult),
            waits=[tri_tok], inc="sg")
        iota_tok = p.emit("gpsimd",
                          lambda e: e.iota(pvec[:], pattern=[[0, 1]],
                                           channel_multiplier=1, base=-127,
                                           allow_small_or_imprecise_dtypes=True),
                          inc="sg")
        evec_tok = [None] * NH
        for h in range(NH):
            evec_tok[h] = p.emit(
                "scalar",
                lambda e, h=h: e.activation(evec[:, h:h + 1], pvec[:], AF.Exp,
                                            scale=SLOPES[h]),
                waits=[iota_tok], inc="sc")

        ln_state = {"tr_done": [None] * 2, "ev": None}
        exp_rd0_tok = [None]
        exp_wr0_tok = [None]

        def emit_ln_group(tts, xtoks, xaps, rd_out=None):
            t_sum = None
            t_sq = None
            for j, tt in enumerate(tts):
                xt = xaps[j]
                t_sum = p.emit("vector",
                               lambda e, tt=tt, xt=xt: e.reduce_sum(
                                   stats[:, 0, tt:tt + 1], xt, axis=AX.X),
                               waits=[xtoks[j]], inc="sv")
                t_sq = p.emit("scalar",
                              lambda e, tt=tt, xt=xt: e.activation(
                                  sq_s[:], xt, AF.Square,
                                  accum_out=stats[:, 1, tt:tt + 1]),
                              waits=[xtoks[j]], inc="sc")
            g0 = tts[0]
            sl = slice(g0, g0 + len(tts))
            m1 = p.emit("vector",
                        lambda e: e.tensor_scalar(stats[:, 2, sl], stats[:, 0, sl],
                                                  1.0 / D, None, ALU.mult),
                        waits=[t_sum], inc="sv")
            m2 = p.emit("vector",
                        lambda e: e.tensor_tensor(stats[:, 3, sl], stats[:, 2, sl],
                                                  stats[:, 2, sl], ALU.mult),
                        waits=[m1], inc="sv")
            m3 = p.emit("vector",
                        lambda e: e.tensor_scalar(stats[:, 4, sl], stats[:, 1, sl],
                                                  1.0 / D, EPS, ALU.mult, ALU.add),
                        waits=[t_sq], inc="sv")
            m4 = p.emit("vector",
                        lambda e: e.tensor_tensor(stats[:, 4, sl], stats[:, 4, sl],
                                                  stats[:, 3, sl], ALU.subtract),
                        waits=[m2, m3], inc="sv")
            m5 = p.emit("scalar",
                        lambda e: e.activation(stats[:, 3, sl], stats[:, 4, sl],
                                               AF.Sqrt),
                        waits=[m4], inc="sc")
            m6 = p.emit("vector",
                        lambda e: e.reciprocal(stats[:, 4, sl], stats[:, 3, sl]),
                        waits=[m5], inc="sv")
            ev_toks = []
            for j, tt in enumerate(tts):
                xt = xaps[j]
                sl2 = j % 2
                t_xn = p.emit("vector",
                              lambda e, sl2=sl2, tt=tt, xt=xt: e.tensor_scalar(
                                  xn_tm[:, sl2, :], xt,
                                  stats[:, 2, tt:tt + 1],
                                  stats[:, 4, tt:tt + 1],
                                  ALU.subtract, ALU.mult),
                              waits=[m6, ln_state["tr_done"][sl2]], inc="sv")
                ev_tok = None
                for half in range(2):
                    te_tok = None
                    for j4 in range(4):
                        jj = half * 4 + j4
                        te_tok = p.emit(
                            "tensor",
                            lambda e, sl2=sl2, jj=jj, j4=j4: e.matmul(
                                ps_tr[:, j4 * 128:j4 * 128 + 128],
                                xn_tm[:, sl2, jj * 128:jj * 128 + 128],
                                ident_bf[:], start=True, stop=True),
                            waits=[t_xn, ident_tok, ln_state["ev"]], inc="st")
                    ev_tok = p.emit(
                        "scalar",
                        lambda e, half=half, tt=tt: e.activation(
                            xn_fm[:, half * 4:half * 4 + 4, tt * 128:tt * 128 + 128],
                            ps_tr[:].rearrange("p (c h) -> p c h", c=4),
                            AF.Identity),
                        waits=[te_tok], inc="sc")
                    ln_state["ev"] = ev_tok
                ln_state["tr_done"][sl2] = te_tok
                ev_toks.append(ev_tok)
                if rd_out is not None:
                    rd_out.append((t_xn, t_sq))
            return ev_toks

        xn1_tok = [None] * NT
        vready = [None] * NT
        _va_cm = nc.sbuf_tensor("v_aug", [128, NT, NH * 65], BF16)
        v_aug = _va_cm.__enter__()
        if True:
            vones_tok = p.emit(
                "vector",
                lambda e: e.memset(
                    v_aug[:].rearrange("p t (h c) -> p t h c", c=65)
                    [:, :, :, 64:65], 1.0),
                inc="sv")

            def wtile(m):
                return m.rearrange("(kc p) d -> p kc d", p=128)

            _k_cm = nc.sbuf_tensor("k_all", [128, NKC, S], BF16)
            k_all = _k_cm.__enter__()
            _qh_cm = nc.sbuf_tensor("q_hiT", [128, NKC, S // 2], BF16)
            q_hiT = _qh_cm.__enter__()
            _ql_cm = nc.sbuf_tensor("q_loT", [128, NKC, S // 2], BF16)
            q_loT = _ql_cm.__enter__()
            with nc.sbuf_tensor("x_r", [128, 6, D], F32) as x_r, \
                 nc.sbuf_tensor("wv_bf", [128, NKC, D], BF16) as wv_bf, \
                 nc.sbuf_tensor("vbr_bf", [1, D], BF16) as vbr_bf:
                vbr_tok = p.emit("gpsimd",
                                 lambda e: e.dma_start(out=vbr_bf[:], in_=vbr_h),
                                 inc="skg", inc_by=16)
                x_tok = [None] * 8
                xn_rd = {}

                def emit_x_pair(j):
                    sem = "sx" if j % 2 == 0 else "sx2"
                    guards = []
                    for t2 in (2 * j - 6, 2 * j - 5):
                        if t2 >= 0:
                            guards += list(xn_rd[t2])
                    x_tok[j] = p.emit(
                        "sync",
                        lambda e, j=j: e.dma_start(
                            out=x_r[:, (2 * j) % 6:(2 * j) % 6 + 2, :],
                            in_=xq[:, 2 * j:2 * j + 2, :]),
                        waits=guards + [(sem, p.cnt[sem])],
                        inc=sem, inc_by=16)

                for j in range(3):
                    emit_x_pair(j)
                wv_tok = p.emit("gpsimd",
                                lambda e: e.dma_start(out=wv_bf[:], in_=wtile(wvT_h)),
                                waits=[("swa", p.cnt["swa"])],
                                inc="swa", inc_by=16)
                qk_w_tok = {}
                q_tok_d = {}
                k_tok_d = {}
                qk_last = {0: None, 1: None}
                qk_rd = {0: None, 1: None}
                wseq = [0]

                def emit_qk_slices(dc, wait):
                    seq = wseq[0]
                    wseq[0] += 1
                    buf = seq % 2
                    sem = "swa" if buf == 0 else "swb"
                    p.emit("gpsimd",
                           lambda e, dc=dc, buf=buf: e.dma_start(
                               out=wqk[:, buf, 0, :, :],
                               in_=wtile(wqT_h[:, dc * 128:dc * 128 + 128])),
                           waits=[wait, (sem, p.cnt[sem])], inc=sem, inc_by=16)
                    qk_w_tok[seq] = p.emit(
                        "gpsimd",
                        lambda e, dc=dc, buf=buf: e.dma_start(
                            out=wqk[:, buf, 1, :, :],
                            in_=wtile(wkT_h[:, dc * 128:dc * 128 + 128])),
                        inc=sem, inc_by=16)
                    return seq

                prev_ev = {0: None, 1: None}

                def emit_qk_half(tbh, dc):
                    seq = qk_seq[(tbh, dc)]
                    buf = seq % 2
                    for which, dst, bias_sb in ((0, None, qb_s),
                                                (1, k_all, kb_s)):
                        psq = pA if which == 0 else pB
                        mm = None
                        for kc in range(NKC):
                            for tb2 in range(2):
                                tb = tbh * 2 + tb2
                                mm = p.emit(
                                    "tensor",
                                    lambda e, kc=kc, tb=tb, tb2=tb2, psq=psq,
                                    buf=buf, which=which: e.matmul(
                                        psq[:, tb2 * 512:tb2 * 512 + 512],
                                        wqk[:, buf, which, kc, :],
                                        xn_fm[:, kc, tb * 512:tb * 512 + 512],
                                        start=(kc == 0), stop=(kc == NKC - 1)),
                                    waits=[qk_w_tok[seq],
                                           xn1_tok[tbh * 8 + 7],
                                           prev_ev[which]],
                                    inc="st")
                        if which == 0:
                            dstq = q_loT if tbh == 0 else q_hiT
                        else:
                            dstq = None
                        ev = p.emit(
                            "scalar",
                            lambda e, dc=dc, psq=psq, tbh=tbh, dstq=dstq,
                            dst=dst, which=which, bias_sb=bias_sb:
                            e.activation(
                                (dstq[:, dc, :] if which == 0 else
                                 dst[:, dc, tbh * 1024:tbh * 1024 + 1024]),
                                psq[:],
                                AF.Identity, bias=bias_sb[:, dc:dc + 1]),
                            waits=[mm, qb_tok], inc="sc")
                        prev_ev[which] = ev
                        qk_last[buf] = mm
                        qk_rd[buf] = mm
                        if which == 0:
                            q_tok_d[dc] = ev
                        else:
                            k_tok_d[dc] = ev

                qk_seq = {}
                qk_sched = [(0, d) for d in range(NKC)] + \
                           [(1, d) for d in range(NKC)]
                qk_seq[qk_sched[0]] = emit_qk_slices(qk_sched[0][1], None)
                qk_seq[qk_sched[1]] = emit_qk_slices(qk_sched[1][1], None)
                nxt = [2]

                v_ev = [None]
                for g4 in range(4):
                    rd_out = []
                    evs = emit_ln_group(list(range(4 * g4, 4 * g4 + 4)),
                                        [x_tok[(4 * g4 + k2) // 2]
                                         for k2 in range(4)],
                                        [x_r[:, tt % 6, :]
                                         for tt in range(4 * g4, 4 * g4 + 4)],
                                        rd_out)
                    for k2 in range(4):
                        xn_rd[4 * g4 + k2] = rd_out[k2]
                    for j in (3 + 2 * g4, 4 + 2 * g4):
                        if j < 8:
                            emit_x_pair(j)
                    xn1_tok[4 * g4:4 * g4 + 4] = evs
                    for tt in range(4 * g4, 4 * g4 + 4):
                        p.emit("tensor",
                               lambda e: e.matmul(pC[:, 0:512], ones_bf[:],
                                                  vbr_bf[:, 0:512],
                                                  start=True, stop=False),
                               waits=[ones_tok, vbr_tok] +
                               (list(v_ev[0]) if v_ev[0] else []), inc="st")
                        p.emit("tensor",
                               lambda e: e.matmul(pC[:, 512:1024],
                                                  ones_bf[:],
                                                  vbr_bf[:, 512:1024],
                                                  start=True, stop=False),
                               inc="st")
                        tok = None
                        for kc in range(NKC):
                            stp = kc == NKC - 1
                            p.emit("tensor",
                                   lambda e, kc=kc, stp=stp, tt=tt: e.matmul(
                                       pC[:, 0:512],
                                       xn_fm[:, kc, tt * 128:tt * 128 + 128],
                                       wv_bf[:, kc, 0:512], start=False, stop=stp),
                                   waits=[xn1_tok[tt], wv_tok], inc="st")
                            tok = p.emit("tensor",
                                         lambda e, kc=kc, stp=stp, tt=tt:
                                         e.matmul(
                                             pC[:, 512:1024],
                                             xn_fm[:, kc, tt * 128:tt * 128 + 128],
                                             wv_bf[:, kc, 512:1024],
                                             start=False, stop=stp),
                                         inc="st")
                        ev1 = p.emit(
                            "vector",
                            lambda e, tt=tt: e.tensor_copy(
                                v_aug[:, tt, 0:8 * 65].rearrange(
                                    "p (h c) -> p h c", c=65)[:, :, 0:64],
                                pC[:, 0:512].rearrange(
                                    "p (h c) -> p h c", c=64)),
                            waits=[tok, vones_tok], inc="sv")
                        ev2 = p.emit(
                            "scalar",
                            lambda e, tt=tt: e.activation(
                                v_aug[:, tt, 8 * 65:16 * 65].rearrange(
                                    "p (h c) -> p h c", c=65)[:, :, 0:64],
                                pC[:, 512:1024].rearrange(
                                    "p (h c) -> p h c", c=64), AF.Identity),
                            waits=[tok, vones_tok], inc="sc")
                        v_ev[0] = (ev1, ev2)
                        vready[tt] = (ev1, ev2)
                    if g4 >= 1:
                        for _ in range(4 if g4 < 3 else 4):
                            if qk_sched:
                                key = qk_sched.pop(0)
                                emit_qk_half(*key)
                                if nxt[0] < 2 * NKC:
                                    nk = ((0, nxt[0]) if nxt[0] < NKC
                                          else (1, nxt[0] - NKC))
                                    qk_seq[nk] = emit_qk_slices(
                                        nk[1], qk_rd[nxt[0] % 2])
                                    nxt[0] += 1
                v_done, v_done2 = vready[NT - 1]
                while qk_sched:
                    key = qk_sched.pop(0)
                    emit_qk_half(*key)
                    if nxt[0] < 2 * NKC:
                        nk = ((0, nxt[0]) if nxt[0] < NKC
                              else (1, nxt[0] - NKC))
                        qk_seq[nk] = emit_qk_slices(nk[1], qk_rd[nxt[0] % 2])
                        nxt[0] += 1
            if stop < 2:
                return _finish(nc, p)

            ph2_sc = (prev_ev[0], prev_ev[1])
            wo_wqk = wqk[:].rearrange("p a b k c -> p (a b k c)").rearrange(
                "p (k c) -> p k c", c=512)
            wo_wqk_tok = p.emit(
                "gpsimd",
                lambda e: e.dma_start(out=wo_wqk,
                                      in_=wtile(woT_h[:, 0:512])),
                waits=[qk_last[0], qk_last[1]], inc="swo", inc_by=16)

            if stop < 3:
                return _finish(nc, p)

            def group_slot(h, idx, q_lo):
                if h % 2 == 0 and WCH.get(h, 512) == 64:
                    if q_lo % 128 == 64:
                        return pD, 512 + (idx // 2) * 65
                    return pC, (idx // 2) * 65
                return pC, (h % 2) * 512 + idx * 65

            exp_rd = {0: None, 1: None}
            exp_wr = {0: None, 1: None}
            sc_rd = {0: prev_ev[0], 1: prev_ev[1]}
            vs_rd = [None] * 12
            zrec_rd = [None] * 16
            vs_i = [0]
            norm_prev = {}
            pair_norm_last = [None] * NKC
            pending_norms = []

            def flush_norms():
                for fn in pending_norms:
                    fn()
                pending_norms.clear()
            last_scores_tok = [None]
            gstep = [0]

            def q_ref(rows, dc, c0, c1):
                if c1 <= 1024:
                    return q_loT[rows, dc, c0:c1]
                return q_hiT[rows, dc, c0 - 1024:c1 - 1024]

            def ap_ref(ttile):
                if ttile < 8:
                    return ap_lo, ttile
                return ap_hi, ttile - 8

            def attn_block(qb, dc, single_slot, feed):
                q0 = qb * QB
                hA, hB = 2 * dc, 2 * dc + 1
                q_tok = q_tok_d[dc]
                k_tok = k_tok_d[dc]
                groups = []
                for h in (hA, hB):
                    idx = 0
                    for (c0, w, R, klo, khi) in CHUNKS[h]:
                        if c0 >= q0 + QB or c0 + w <= q0:
                            continue
                        if w <= 128:
                            groups.append((h, idx, c0, w, R,
                                           list(range(klo, khi + 1))))
                            idx += 1
                        else:
                            for j in range(c0 // 128, (c0 + w) // 128):
                                groups.append((h, idx, j * 128, 128, R,
                                               list(range(klo, j + 1))))
                                idx += 1
                all_kis = sorted({k for g in groups for k in g[5]})
                act_k = {hA: set(), hB: set()}
                win = {}
                for (h, idx, q_lo_, M, R, kis) in groups:
                    act_k[h].update(kis)
                    for k2 in kis:
                        a, b = win.get((k2, h), (1 << 30, -1))
                        win[(k2, h)] = (min(a, q_lo_), max(b, q_lo_ + M))

                def diag_key(g, ki):
                    h, idx, q_lo_, M, R, kis = g
                    fo = ki * 128 - q0
                    if not (q0 <= ki * 128 < q0 + QB):
                        return False
                    lo = q_lo_ - q0
                    return lo < fo + 128 and lo + M > fo

                glists = {}
                for ki in all_kis:
                    gl = [g for g in groups if ki in g[5]]
                    gl.sort(key=lambda g: diag_key(g, ki))
                    glists[ki] = gl
                bank_first = {}
                bank_last = {}
                for ki in all_kis:
                    for (h, idx, q_lo_, M, R, kis) in glists[ki]:
                        pso, col = group_slot(h, idx, q_lo_)
                        bank = (id(pso), col // 512, q_lo_ % 128)
                        key = (ki, h, idx)
                        bank_first.setdefault(bank, key)
                        bank_last[bank] = key
                vs_map = {}
                ki_av_tok = {}
                etoks = {}
                mtoks = {}

                def emit_scores(i, ki):
                    par = 0 if single_slot else gstep[0] % 2
                    pss = pA if par == 0 else pB
                    eslot = gstep[0] % 2
                    for (h, idx, q_lo_, M, R, kis) in groups:
                        if ki not in kis:
                            continue
                        vkey = (h, R, ki)
                        if vkey in vs_map:
                            continue
                        s2 = math.exp(SLOPES[h] * (ki * 128 + 127 - R))
                        slot = vs_i[0] % 12
                        vs_i[0] += 1
                        veng, vinc = (("gpsimd", "sg") if slot % 2 == 0
                                      else ("vector", "sv"))
                        vtok = p.emit(
                            veng,
                            lambda e, h=h, ki=ki, s2=s2, slot=slot:
                            e.tensor_scalar(
                                vs_s[:, slot, :],
                                v_aug[:, ki, h * 65:h * 65 + 65],
                                evec[:, h:h + 1], float(s2),
                                ALU.mult, ALU.mult),
                            waits=[vready[ki][0], vready[ki][1],
                                   evec_tok[h], vs_rd[slot]],
                            inc=vinc)
                        vs_map[vkey] = (slot, vtok)
                    a_act = ki in act_k[hA]
                    b_act = ki in act_k[hB]
                    sA = sB = None
                    spans = []
                    if a_act:
                        al, ah = win[(ki, hA)]
                        sA = p.emit(
                            "tensor",
                            lambda e, ki=ki, pss=pss, al=al, ah=ah, q0=q0,
                            dc=dc: e.matmul(
                                pss[:, al - q0:ah - q0],
                                k_all[0:64, dc, ki * 128:ki * 128 + 128],
                                q_ref(slice(0, 64), dc, al, ah),
                                start=True, stop=True, tile_position=(0, 0)),
                            waits=[q_tok, k_tok, sc_rd[par]], inc="st")
                        spans.append([al - q0, ah - q0, sA, hA])
                    if b_act:
                        al, ah = win[(ki, hB)]
                        sB = p.emit(
                            "tensor",
                            lambda e, ki=ki, pss=pss, al=al, ah=ah, q0=q0,
                            dc=dc: e.matmul(
                                pss[:, 512 + al - q0:512 + ah - q0],
                                k_all[64:128, dc, ki * 128:ki * 128 + 128],
                                q_ref(slice(64, 128), dc, al, ah),
                                start=True, stop=True, tile_position=(64, 0)),
                            waits=[q_tok, k_tok, sc_rd[par]], inc="st")
                        spans.append([512 + al - q0, 512 + ah - q0, sB, hB])
                    last_scores_tok[0] = sB if sB is not None else sA
                    if len(spans) == 2 and spans[0][1] == spans[1][0]:
                        spans = [[spans[0][0], spans[1][1], sB, None]]
                    etok = None
                    for (c0_, c1_, stok, eh) in spans:
                        etok = p.emit(
                            "scalar",
                            lambda e, pss=pss, eslot=eslot, c0_=c0_, c1_=c1_:
                            e.activation(
                                exp_s[:, eslot, c0_:c1_],
                                pss[:, c0_:c1_], AF.Exp, scale=0.125),
                            waits=[stok, exp_rd[eslot]], inc="sc")
                        if eh is None:
                            etoks[(ki, hA)] = etoks[(ki, hB)] = etok
                        else:
                            etoks[(ki, eh)] = etok
                    sc_rd[par] = etok
                    exp_wr[eslot] = etok
                    if eslot == 0:
                        exp_wr0_tok[0] = etok
                    etoks[ki] = etok
                    if q0 <= ki * 128 < q0 + QB:
                        fo = ki * 128 - q0
                        mtok = None
                        for hh in ([0] if not b_act else
                                   ([1] if not a_act else [0, 1])):
                            mtok = p.emit(
                                "vector",
                                lambda e, eslot=eslot, fo=fo, hh=hh:
                                e.tensor_tensor(
                                    exp_s[:, eslot,
                                          hh * 512 + fo:hh * 512 + fo + 128],
                                    exp_s[:, eslot,
                                          hh * 512 + fo:hh * 512 + fo + 128],
                                    tri_bf[:], ALU.mult),
                                waits=[etoks[(ki, hA if hh == 0 else hB)],
                                       tri_tok], inc="sv")
                        mtoks[ki] = mtok

                def emit_av(i, ki, eslot):
                    mtok = mtoks.get(ki)
                    av_tok = None

                    def is_diag(g):
                        return mtok is not None and diag_key(g, ki)

                    for g in glists[ki]:
                        h, idx, q_lo_, M, R, kis = g
                        etok = etoks[(ki, h)]
                        slot, vtok = vs_map[(h, R, ki)]
                        pso, col = group_slot(h, idx, q_lo_)
                        bank = (id(pso), col // 512, q_lo_ % 128)
                        is_first = bank_first[bank] == (ki, h, idx)
                        is_last = bank_last[bank] == (ki, h, idx)
                        rowb = q_lo_ % 128
                        hcol = (h % 2) * 512 + (q_lo_ - q0)
                        tp = (0, 64) if rowb == 64 else None
                        av_tok = p.emit(
                            "tensor",
                            lambda e, eslot=eslot, hcol=hcol, M=M, pso=pso,
                            col=col, rowb=rowb, slot=slot, f=is_first,
                            l=is_last, tp=tp: e.matmul(
                                pso[rowb:rowb + M, col:col + 65],
                                exp_s[:, eslot, hcol:hcol + M],
                                vs_s[:, slot, :],
                                start=f, stop=l, tile_position=tp),
                            waits=[etok, mtok if is_diag(g) else None,
                                   vtok,
                                   norm_prev.get(bank, v_done),
                                   None if bank in norm_prev else v_done2],
                            inc="st")
                        vs_rd[slot] = av_tok
                        ki_av_tok[ki] = av_tok
                    if av_tok is not None:
                        exp_rd[eslot] = av_tok
                        if eslot == 0:
                            exp_rd0_tok[0] = av_tok

                prev = None
                for i, ki in enumerate(all_kis):
                    if feed is not None:
                        feed()
                    emit_scores(i, ki)
                    if i == 0:
                        flush_norms()
                    if prev is not None:
                        emit_av(*prev)
                    prev = (i, ki, gstep[0] % 2)
                    gstep[0] += 1
                emit_av(*prev)
                qb_av_done = ki_av_tok[all_kis[-1]]
                pending_norms.append(
                    lambda qb_av_done=qb_av_done, groups=groups, dc=dc:
                    emit_norms(qb_av_done, groups, dc))
                return
            def emit_norms(qb_av_done, groups, dc):
                runs = {}
                for gi, (h, idx, q_lo_, M, R, kis) in enumerate(groups):
                    pso, col = group_slot(h, idx, q_lo_)
                    key = (id(pso), col // 512, q_lo_ % 128)
                    runs.setdefault(key, (pso, []))[1].append((col, gi))
                zslot_of = {}
                rtok_of = {}
                zi = 0
                for (kid, chalf, rowb), (pso, cols) in runs.items():
                    cols.sort()
                    c0 = cols[0][0]
                    n = len(cols)
                    M = 64 if rowb == 64 else (
                        64 if WCH.get(groups[cols[0][1]][0], 512) == 64 else 128)
                    zs = [None] * n
                    for j, (c, gi) in enumerate(cols):
                        zs[j] = zi % 16
                        zslot_of[gi] = zi % 16
                        zi += 1
                    assert zs == list(range(zs[0], zs[0] + n)), zs
                    rt = p.emit(
                        "vector",
                        lambda e, pso=pso, c0=c0, n=n, rowb=rowb, M=M, z0=zs[0]:
                        e.reciprocal(
                            zrec[rowb:rowb + M, z0:z0 + n],
                            pso[rowb:rowb + M, c0:c0 + n * 65].rearrange(
                                "p (n c) -> p n c", c=65)[:, :, 64:65]
                            .rearrange("p n c -> p (n c)")
                            if n > 1 else
                            pso[rowb:rowb + M, c0 + 64:c0 + 65]),
                        waits=[qb_av_done] + [zrec_rd[z] for z in zs],
                        inc="sv")
                    for (c, gi) in cols:
                        rtok_of[gi] = rt
                for gi, (h, idx, q_lo_, M, R, kis) in enumerate(groups):
                    pso, col = group_slot(h, idx, q_lo_)
                    bank = (id(pso), col // 512, q_lo_ % 128)
                    rowb = q_lo_ % 128
                    zslot = zslot_of[gi]
                    rtok = rtok_of[gi]
                    apb, att = ap_ref(q_lo_ // 128)
                    ntok = p.emit(
                        "vector",
                        lambda e, pso=pso, col=col, rowb=rowb, M=M, zslot=zslot,
                        h=h, dc=dc, apb=apb, att=att: e.tensor_scalar(
                            apb[rowb:rowb + M, att,
                                dc * 128 + (h % 2) * 64:
                                dc * 128 + (h % 2) * 64 + 64],
                            pso[rowb:rowb + M, col:col + 64],
                            zrec[rowb:rowb + M, zslot:zslot + 1], None,
                            ALU.mult),
                        waits=[rtok], inc="sv")
                    zrec_rd[zslot] = ntok
                    norm_prev[bank] = ntok
                    pair_norm_last[dc] = ntok

            _aph_cm = nc.sbuf_tensor("ap_hi", [128, 8, D], BF16, side="right")
            ap_hi = _aph_cm.__enter__()
            _apl_cm = nc.sbuf_tensor("ap_lo", [128, 8, D], BF16, side="right")
            ap_lo = _apl_cm.__enter__()
            hold = {}

            for qb in (0, 1):
                for dc in range(NKC):
                    attn_block(qb, dc, False, None)
            flush_norms()
            pnl01 = list(pair_norm_last)
            pb_guard = sc_rd[1]
            q_lo_last = last_scores_tok[0]
            _ql_cm.__exit__(None, None, None)
            if stop < 4:
                return _finish(nc, p)

            wo_toks = {}
            wo_rd = [None]

            def emit_wo_load(half, wait):
                wo_h = hold["wo_h"]
                wo_toks[half] = p.emit(
                    "gpsimd",
                    lambda e, half=half, wo_h=wo_h: e.dma_start(
                        out=wo_h[:],
                        in_=wtile(woT_h[:, half * 512:half * 512 + 512])),
                    waits=[wait, ("swo", p.cnt["swo"])], inc="swo", inc_by=16)

            radd_tok = [None] * NT
            xn2_tok = [None] * NT

            def gen_th0():
                for dcc in range(NKC):
                    for half in range(2):
                        te_tok = None
                        for j4 in range(4):
                            tt = half * 4 + j4
                            te_tok = p.emit(
                                "tensor",
                                lambda e, tt=tt, j4=j4, dcc=dcc: e.matmul(
                                    pB[:, j4 * 128:j4 * 128 + 128],
                                    ap_lo[:, tt, dcc * 128:dcc * 128 + 128],
                                    ident_bf[:], start=True, stop=True),
                                waits=[pnl01[dcc], ident_tok, pb_guard,
                                       ln_state["ev"]], inc="st")
                            yield
                        ev = p.emit(
                            "vector",
                            lambda e, dcc=dcc, half=half: e.tensor_copy(
                                xn_fm[:, dcc, half * 512:half * 512 + 512],
                                pB[:, 0:512]),
                            waits=[te_tok], inc="sv")
                        ln_state["ev"] = ev
                        yield
                tr0_done = ln_state["ev"]
                _apl_cm.__exit__(None, None, None)
                _wo_cm = nc.sbuf_tensor("wo_h", [128, NKC, 512], BF16,
                                        side="right")
                hold["wo_h"] = _wo_cm.__enter__()
                _x2h0_cm = nc.sbuf_tensor("x2h0", [128, 8, D], F32,
                                          side="right")
                x2h0 = _x2h0_cm.__enter__()
                hold["x2h0"] = x2h0
                hold["x2h0_cm"] = _x2h0_cm
                x2h0_tok1 = p.emit("sync",
                                   lambda e: e.dma_start(out=x2h0[:, 0:6, :],
                                                         in_=xq[:, 0:6, :]),
                                   waits=[q_lo_last, v_done, v_done2, ph2_sc[0],
                                          ph2_sc[1]],
                                   inc="sx3", inc_by=16)
                x2h0_tok = p.emit("sync",
                                  lambda e: e.dma_start(out=x2h0[:, 6:8, :],
                                                        in_=xq[:, 6:8, :]),
                                  waits=[tr0_done, ("sx3", p.cnt["sx3"])],
                                  inc="sx3", inc_by=16)
                emit_wo_load(1, tr0_done)
                y_ev = {0: None, 1: None}
                for half in range(2):
                    wsrc = wo_wqk if half == 0 else hold["wo_h"][:]
                    wtok = wo_wqk_tok if half == 0 else wo_toks[1]
                    for tt in range(8):
                        psy = pB[:, 0:512] if tt % 2 == 0 else pB[:, 512:1024]
                        p.emit("tensor",
                               lambda e, psy=psy, half=half, wtok=wtok:
                               e.matmul(
                                   psy, ones_bf[:],
                                   obr_bf[:, half * 512:half * 512 + 512],
                                   start=True, stop=False),
                               waits=[obr_tok, tr0_done, y_ev[tt % 2],
                                      wtok], inc="st")
                        mm = None
                        for kc in range(NKC):
                            mm = p.emit(
                                "tensor",
                                lambda e, kc=kc, tt=tt, psy=psy,
                                wsrc=wsrc: e.matmul(
                                    psy,
                                    xn_fm[:, kc, tt * 128:tt * 128 + 128],
                                    wsrc[:, kc, :],
                                    start=False, stop=(kc == NKC - 1)),
                                inc="st")
                            yield
                        wo_rd[0] = mm
                        radd = p.emit(
                            "vector",
                            lambda e, tt=tt, psy=psy, half=half:
                            e.tensor_tensor(
                                x2h0[:, tt, half * 512:half * 512 + 512],
                                x2h0[:, tt, half * 512:half * 512 + 512],
                                psy, ALU.add),
                            waits=[mm, x2h0_tok1 if tt < 6 else x2h0_tok],
                            inc="sv")
                        y_ev[tt % 2] = radd
                        if half == 1:
                            radd_tok[tt] = radd
                        yield
                    if half == 0:
                        hold["wo_rd_h0"] = wo_rd[0]
                for g4o in range(2):
                    evs = emit_ln_group(list(range(4 * g4o, 4 * g4o + 4)),
                                        radd_tok[4 * g4o:4 * g4o + 4],
                                        [x2h0[:, tt, :]
                                         for tt in range(4 * g4o, 4 * g4o + 4)])
                    xn2_tok[4 * g4o:4 * g4o + 4] = evs
                    yield

            gen = gen_th0()
            fed = [0]

            def feed():
                for _ in range(3):
                    if next(gen, "END") == "END":
                        break

            order = [(2, 0), (3, 0)] + [(2, d) for d in range(1, NKC)] + \
                    [(3, d) for d in range(1, NKC)]
            for qb, dc in order:
                attn_block(qb, dc, False, None)
            flush_norms()
            pb_guard = sc_rd[1]
            for _ in gen:
                pass
            pnl = list(pair_norm_last)

            x2h0 = hold["x2h0"]
            ve_snap = ("sv", p.cnt["sv"])
            pa_guard = sc_rd[0]
            qk_post = last_scores_tok[0]
            _qh_cm.__exit__(None, None, None)
            _k_cm.__exit__(None, None, None)
            _va_cm.__exit__(None, None, None)
            if stop < 5:
                return _finish(nc, p)

            with nc.sbuf_tensor("h_fm", [128, NHC, S // 2], BF16) as h_fm, \
                 nc.sbuf_tensor("wd_bf", [128, 2, NHC, 256], BF16) as wd_bf:
                wgu = wqk
                wgT_q = wgT_h.rearrange("(kc p) hh -> p kc hh", p=128)
                wuT_q = wuT_h.rearrange("(kc p) hh -> p kc hh", p=128)
                gu_tok = {}

                def emit_gu_load(m, wait):
                    hc = m % NHC
                    buf = m % 2
                    sem = "sfa" if buf == 0 else "sfb"
                    p.emit("gpsimd",
                           lambda e, hc=hc, buf=buf: e.dma_start(
                               out=wgu[:, buf, 0, :, :],
                               in_=wgT_q[:, :, hc * 128:hc * 128 + 128]),
                           waits=[wait], inc=sem, inc_by=16)
                    gu_tok[m] = p.emit(
                        "gpsimd",
                        lambda e, hc=hc, buf=buf: e.dma_start(
                            out=wgu[:, buf, 1, :, :],
                            in_=wuT_q[:, :, hc * 128:hc * 128 + 128]),
                        inc=sem, inc_by=16)

                emit_gu_load(0, hold["wo_rd_h0"])
                emit_gu_load(1, hold["wo_rd_h0"])
                emit_wo_load(0, wo_rd[0])
                wdT_q = wdT_h.rearrange("(hc p) d -> p hc d", p=128)
                down_ev = {}
                sil_prev = {0: None, 1: None}
                mul_prev = {0: None, 1: None}
                fst = {"wd_rd": None, "down_done": None, "h_done": None,
                       "last_store": None}

                def emit_ffn_half(th):
                    tbase = th * 1024
                    g_ev = None
                    for hc in range(NHC):
                        m = th * NHC + hc
                        buf = m % 2
                        par = hc % 2
                        pg = pA if par == 0 else pB
                        pu = pC if par == 0 else pD
                        mmg = None
                        for kc in range(NKC):
                            for tb2 in range(2):
                                mmg = p.emit(
                                    "tensor",
                                    lambda e, kc=kc, tb2=tb2, buf=buf,
                                    tbase=tbase, pg=pg: e.matmul(
                                        pg[:, tb2 * 512:tb2 * 512 + 512],
                                        wgu[:, buf, 0, kc, :],
                                        xn_fm[:, kc, tbase + tb2 * 512:
                                              tbase + tb2 * 512 + 512],
                                        start=(kc == 0), stop=(kc == NKC - 1)),
                                    waits=[gu_tok[m], xn2_tok[th * 8 + 7],
                                           sil_prev[par], fst["down_done"],
                                           pa_guard, ve_snap], inc="st")
                        mmu = None
                        for kc in range(NKC):
                            for tb2 in range(2):
                                mmu = p.emit(
                                    "tensor",
                                    lambda e, kc=kc, tb2=tb2, buf=buf,
                                    tbase=tbase, pu=pu: e.matmul(
                                        pu[:, tb2 * 512:tb2 * 512 + 512],
                                        wgu[:, buf, 1, kc, :],
                                        xn_fm[:, kc, tbase + tb2 * 512:
                                              tbase + tb2 * 512 + 512],
                                        start=(kc == 0), stop=(kc == NKC - 1)),
                                    waits=[mul_prev[par], ve_snap,
                                           xn2_tok[th * 8 + 7] if par == 1
                                           else None], inc="st")
                        s1 = p.emit("scalar",
                                    lambda e, par=par, pg=pg: e.activation(
                                        exp_s[:, par, :],
                                        pg[:], AF.Silu, scale=QSCALE),
                                    waits=[mmg, exp_rd[0], exp_rd[1],
                                           mul_prev[par]], inc="sc")
                        sil_prev[par] = s1
                        g_ev = p.emit(
                            "vector",
                            lambda e, hc=hc, par=par, pu=pu:
                            e.scalar_tensor_tensor(
                                h_fm[:, hc, :], pu[:], QSCALE,
                                exp_s[:, par, :], ALU.mult, ALU.mult),
                            waits=[mmu, s1, qk_post], inc="sv")
                        mul_prev[par] = g_ev
                        if m + 2 < 2 * NHC:
                            emit_gu_load(m + 2, mmu)
                    return g_ev

                wd_tok = {}
                wd_rdb = {0: None, 1: None}

                def emit_wd_load(q, wait):
                    buf = q % 2
                    wd_tok[q] = p.emit(
                        "gpsimd",
                        lambda e, q=q, buf=buf: e.dma_start(
                            out=wd_bf[:, buf, :, :],
                            in_=wdT_q[:, :, q * 256:q * 256 + 256]),
                        waits=[wait, qk_post, ("sd", p.cnt["sd"])],
                        inc="sd", inc_by=16)

                def emit_down(th, xbuf):
                    for q in range(4):
                        buf = q % 2
                        mm = None
                        for tt8 in range(8):
                            tt = th * 8 + tt8
                            rkey = (tt8 % 2, q % 2)
                            psd = pB[:, (tt8 % 2) * 512 + (q % 2) * 256:
                                     (tt8 % 2) * 512 + (q % 2) * 256 + 256]
                            mm = None
                            for hc in range(NHC):
                                mm = p.emit(
                                    "tensor",
                                    lambda e, hc=hc, tt8=tt8, psd=psd, buf=buf:
                                    e.matmul(
                                        psd,
                                        h_fm[:, hc, tt8 * 128:tt8 * 128 + 128],
                                        wd_bf[:, buf, hc, :],
                                        start=(hc == 0), stop=(hc == NHC - 1)),
                                    waits=[wd_tok[q], fst["h_done"],
                                           sil_prev[1], down_ev.get(rkey)],
                                    inc="st")
                            ev = p.emit(
                                "vector",
                                lambda e, tt8=tt8, q=q, psd=psd,
                                xbuf=xbuf: e.scalar_tensor_tensor(
                                    xbuf[:, tt8, q * 256:q * 256 + 256],
                                    psd, QSCALE,
                                    xbuf[:, tt8, q * 256:q * 256 + 256],
                                    ALU.mult, ALU.add),
                                waits=[mm], inc="sv")
                            down_ev[rkey] = ev
                            fst["down_done"] = ev
                            fst["last_store"] = p.emit(
                                "sync",
                                lambda e, tt=tt, tt8=tt8, q=q, xbuf=xbuf:
                                e.dma_start(
                                    out=out_h[tt * 128:tt * 128 + 128,
                                              q * 256:q * 256 + 256],
                                    in_=xbuf[:, tt8, q * 256:q * 256 + 256]),
                                waits=[ev], inc="so", inc_by=16)
                        wd_rdb[buf] = mm
                        if q + 2 < 4:
                            emit_wd_load(q + 2, mm)

                emit_wd_load(0, wd_rdb[0])
                emit_wd_load(1, wd_rdb[1])
                fst["h_done"] = emit_ffn_half(0)
                emit_down(0, x2h0)
                if stop < 7:
                    p.final_wait("sync", fst["last_store"])
                    return _finish(nc, p)
                hold["x2h0_cm"].__exit__(None, None, None)

                evg = fst["down_done"]
                for dcc in range(NKC):
                    for half in range(2):
                        te_tok = None
                        for j4 in range(4):
                            lt = half * 4 + j4
                            te_tok = p.emit(
                                "tensor",
                                lambda e, lt=lt, j4=j4, dcc=dcc: e.matmul(
                                    pB[:, j4 * 128:j4 * 128 + 128],
                                    ap_hi[:, lt, dcc * 128:dcc * 128 + 128],
                                    ident_bf[:], start=True, stop=True),
                                waits=[pnl[dcc], ident_tok, evg,
                                       sil_prev[1]], inc="st")
                        evg = p.emit(
                            "vector",
                            lambda e, dcc=dcc, half=half: e.tensor_copy(
                                xn_fm[:, dcc, 1024 + half * 512:
                                      1024 + half * 512 + 512],
                                pB[:, 0:512]),
                            waits=[te_tok], inc="sv")
                tr1_done = evg

                _x2h1_cm = nc.sbuf_tensor("x2h1", [128, 8, D], F32)
                x2h1 = _x2h1_cm.__enter__()
                x2h1_tok1 = p.emit(
                    "scalar",
                    lambda e: e.dma_start(out=x2h1[:, 0:2, :],
                                          in_=xq[:, 8:10, :]),
                    waits=[v_done, v_done2, qk_post, ph2_sc[0], ph2_sc[1],
                           ("sxl", p.cnt["sxl"])],
                    inc="sxl", inc_by=16)
                x2h1_tok = p.emit(
                    "scalar",
                    lambda e: e.dma_start(out=x2h1[:, 2:8, :],
                                          in_=xq[:, 10:16, :]),
                    waits=[fst["down_done"], sil_prev[1],
                           fst["last_store"], ("sxl", p.cnt["sxl"])],
                    inc="sxl", inc_by=16)
                y_ev = {0: None, 1: None}
                for half in range(2):
                    if half == 1:
                        emit_wo_load(1, wo_rd[0])
                    for tt8 in range(8):
                        tt = 8 + tt8
                        psy = pB[:, 0:512] if tt8 % 2 == 0 else pB[:, 512:1024]
                        p.emit("tensor",
                               lambda e, psy=psy, half=half: e.matmul(
                                   psy, ones_bf[:],
                                   obr_bf[:, half * 512:half * 512 + 512],
                                   start=True, stop=False),
                               waits=[obr_tok, tr1_done, y_ev[tt8 % 2],
                                      wo_toks[half]], inc="st")
                        mm = None
                        for kc in range(NKC):
                            mm = p.emit(
                                "tensor",
                                lambda e, kc=kc, tt=tt, psy=psy,
                                wo_h=hold["wo_h"]: e.matmul(
                                    psy,
                                    xn_fm[:, kc, tt * 128:tt * 128 + 128],
                                    wo_h[:, kc, :],
                                    start=False, stop=(kc == NKC - 1)),
                                inc="st")
                        wo_rd[0] = mm
                        radd = p.emit(
                            "vector",
                            lambda e, tt8=tt8, psy=psy, half=half:
                            e.tensor_tensor(
                                x2h1[:, tt8, half * 512:half * 512 + 512],
                                x2h1[:, tt8, half * 512:half * 512 + 512],
                                psy, ALU.add),
                            waits=[mm, x2h1_tok1 if tt8 < 2 else x2h1_tok],
                            inc="sv")
                        y_ev[tt8 % 2] = radd
                        if half == 1:
                            radd_tok[tt] = radd
                ln_state["ev"] = mul_prev[1]
                for g4o in (2, 3):
                    evs = emit_ln_group(list(range(4 * g4o, 4 * g4o + 4)),
                                        radd_tok[4 * g4o:4 * g4o + 4],
                                        [x2h1[:, tt - 8, :]
                                         for tt in range(4 * g4o, 4 * g4o + 4)])
                    xn2_tok[4 * g4o:4 * g4o + 4] = evs

                if stop >= 8:
                    emit_wd_load(0, wd_rdb[0])
                    emit_wd_load(1, wd_rdb[1])
                    fst["h_done"] = emit_ffn_half(1)
                    emit_down(1, x2h1)
                p.final_wait("sync", fst["last_store"])
                _x2h1_cm.__exit__(None, None, None)

        return _finish(nc, p)


def kernel(L0_attn_x, L0_attn_q_w, L0_attn_q_b, L0_attn_k_w, L0_attn_k_b,
           L0_attn_v_w, L0_attn_v_b, L0_attn_o_w, L0_attn_o_b,
           L0_ffn_w_gate, L0_ffn_w_up, L0_ffn_w_down):
    x = np.ascontiguousarray(np.asarray(L0_attn_x, dtype=np.float32))
    wqT = np.ascontiguousarray(np.asarray(L0_attn_q_w, np.float32).T)
    wkT = np.ascontiguousarray(np.asarray(L0_attn_k_w, np.float32).T)
    wvT = np.ascontiguousarray(np.asarray(L0_attn_v_w, np.float32).T)
    woT = np.ascontiguousarray(np.asarray(L0_attn_o_w, np.float32).T)
    qb2 = np.ascontiguousarray(np.asarray(L0_attn_q_b, np.float32).reshape(NKC, 128).T)
    kb2 = np.ascontiguousarray(np.asarray(L0_attn_k_b, np.float32).reshape(NKC, 128).T)
    vbr = np.ascontiguousarray(np.asarray(L0_attn_v_b, np.float32).reshape(1, D))
    obr = np.ascontiguousarray(np.asarray(L0_attn_o_b, np.float32).reshape(1, D))
    wgT = np.ascontiguousarray(np.asarray(L0_ffn_w_gate, np.int32).T)
    wuT = np.ascontiguousarray(np.asarray(L0_ffn_w_up, np.int32).T)
    wdT = np.ascontiguousarray(np.asarray(L0_ffn_w_down, np.int32).T)
    ident = np.eye(128, dtype=np.float32)
    pp, ff = np.mgrid[0:128, 0:128]
    tri = (pp <= ff).astype(np.float32)

    nc = build_program()
    shared = dict(wqT=wqT, wkT=wkT, wvT=wvT, woT=woT, qb2=qb2, kb2=kb2,
                  vbr=vbr, obr=obr, wgT=wgT, wuT=wuT, wdT=wdT,
                  ident=ident, tri=tri)
    in_maps = [dict(x=np.ascontiguousarray(x[c]), **shared) for c in range(8)]
    res = run_bass_kernel_spmd(nc, in_maps, core_ids=list(range(8)))
    return np.stack([res.results[c]["out"] for c in range(8)], axis=0)


# version 38
# speedup vs baseline: 1.0156x; 1.0156x over previous
import contextlib
import math

import numpy as np

import concourse.bass as bass
import concourse.mybir as mybir
from concourse.bass_utils import run_bass_kernel_spmd

F32 = mybir.dt.float32
BF16 = mybir.dt.bfloat16
I32 = mybir.dt.int32
AF = mybir.ActivationFunctionType
ALU = mybir.AluOpType
AX = mybir.AxisListType

S = 2048
D = 1024
NH = 16
HD = 64
HID = 3072
NT = S // 128
NKC = D // 128
NHC = HID // 128
QB = 512
NQB = S // QB
EPS = 1e-5
QSCALE = 1.0 / 127.0

SLOPES = [2.0 ** (-0.5 * (h + 1)) for h in range(NH)]
WCH = {0: 64, 1: 128, 2: 128, 3: 256, 4: 256}
BAND_LOG = 38.0


def _chunks(h):
    w = WCH.get(h, 512)
    B = BAND_LOG / SLOPES[h]
    out = []
    for c0 in range(0, S, w):
        R = c0 + w - 1
        ki_lo = max(0, int(math.ceil((c0 - B - 127) / 128.0)))
        ki_hi = R // 128
        out.append((c0, w, R, ki_lo, ki_hi))
    return out


CHUNKS = [_chunks(h) for h in range(NH)]


class Prog:
    ENGS = ("sync", "tensor", "scalar", "vector", "gpsimd")

    def __init__(self, nc, ctx):
        self.nc = nc
        self.q = {e: [] for e in self.ENGS}
        self.sem = {}
        self.cnt = {}
        self.owner = {}
        self.wm = {e: {} for e in self.ENGS}
        self.ctx = ctx

    def add_sem(self, name, owner=None):
        self.sem[name] = self.ctx.enter_context(self.nc.semaphore(name))
        self.cnt[name] = 0
        self.owner[name] = owner

    def emit(self, eng, fn, waits=(), inc=None, inc_by=1):
        wlist = {}
        for tk in waits:
            if tk is None:
                continue
            s, v = tk
            if v <= 0 or self.owner.get(s) == eng:
                continue
            if self.wm[eng].get(s, 0) >= v:
                continue
            wlist[s] = max(wlist.get(s, 0), v)
        for s, v in wlist.items():
            self.wm[eng][s] = v
        chain = None
        if (inc is not None and eng != "tensor"
                and self.owner.get(inc) == eng and self.cnt[inc] > 0):
            chain = (inc, self.cnt[inc])
        tok = None
        if inc is not None:
            self.cnt[inc] += inc_by
            tok = (inc, self.cnt[inc])
        sems = self.sem

        def op(e, _fn=fn, _w=wlist, _inc=inc, _by=inc_by, _ch=chain):
            for s, v in _w.items():
                e.wait_ge(sems[s], v)
            ins = _fn(e)
            if _ch is not None:
                ins._wait_ge(sems[_ch[0]], _ch[1])
            if _inc is not None:
                ins.then_inc(sems[_inc], _by)

        self.q[eng].append(op)
        return tok

    def final_wait(self, eng, tok):
        s, v = tok
        sems = self.sem
        self.q[eng].append(lambda e: e.wait_ge(sems[s], v))




def _finish(nc, p):
    with nc.Block() as block:
        @block.sync
        def _(e):
            for f in p.q["sync"]:
                f(e)

        @block.tensor
        def _(e):
            for f in p.q["tensor"]:
                f(e)

        @block.scalar
        def _(e):
            for f in p.q["scalar"]:
                f(e)

        @block.vector
        def _(e):
            for f in p.q["vector"]:
                f(e)

        @block.gpsimd
        def _(e):
            for f in p.q["gpsimd"]:
                f(e)
    return nc

def build_program(stop=9):
    nc = bass.Bass("TRN2", target_bir_lowering=False)

    x_h = nc.dram_tensor("x", [S, D], F32, kind="ExternalInput").ap()
    wqT_h = nc.dram_tensor("wqT", [D, D], F32, kind="ExternalInput").ap()
    wkT_h = nc.dram_tensor("wkT", [D, D], F32, kind="ExternalInput").ap()
    wvT_h = nc.dram_tensor("wvT", [D, D], F32, kind="ExternalInput").ap()
    woT_h = nc.dram_tensor("woT", [D, D], F32, kind="ExternalInput").ap()
    qb2_h = nc.dram_tensor("qb2", [128, NKC], F32, kind="ExternalInput").ap()
    kb2_h = nc.dram_tensor("kb2", [128, NKC], F32, kind="ExternalInput").ap()
    vbr_h = nc.dram_tensor("vbr", [1, D], F32, kind="ExternalInput").ap()
    obr_h = nc.dram_tensor("obr", [1, D], F32, kind="ExternalInput").ap()
    wgT_h = nc.dram_tensor("wgT", [D, HID], I32, kind="ExternalInput").ap()
    wuT_h = nc.dram_tensor("wuT", [D, HID], I32, kind="ExternalInput").ap()
    wdT_h = nc.dram_tensor("wdT", [HID, D], I32, kind="ExternalInput").ap()
    ident_h = nc.dram_tensor("ident", [128, 128], F32, kind="ExternalInput").ap()
    tri_h = nc.dram_tensor("tri", [128, 128], F32, kind="ExternalInput").ap()
    out_h = nc.dram_tensor("out", [S, D], F32, kind="ExternalOutput").ap()

    with contextlib.ExitStack() as ctx:
        def sb(name, shape, dt):
            return ctx.enter_context(nc.sbuf_tensor(name, shape, dt))

        def pst(name, shape):
            return ctx.enter_context(nc.psum_tensor(name, shape, F32))

        xn_fm = sb("xn_fm", [128, NKC, S], BF16)
        ident_bf = sb("ident_bf", [128, 128], BF16)
        tri_bf = sb("tri_bf", [128, 128], BF16)
        qb_s = sb("qb_s", [128, NKC], F32)
        kb_s = sb("kb_s", [128, NKC], F32)
        obr_bf = sb("obr_bf", [1, D], BF16)
        ones_bf = sb("ones_bf", [1, 128], BF16)
        pvec = sb("pvec", [128, 1], F32)
        evec = sb("evec", [128, NH], F32)
        stats = sb("stats", [128, 5, 16], F32)
        xn_tm = sb("xn_tm", [128, 2, D], BF16)
        zrec = sb("zrec", [128, 16], F32)
        vs_s = sb("vs_s", [128, 12, 65], BF16)
        exp_s = sb("exp_s", [128, 2, 1024], BF16)
        trineg = sb("trineg", [128, 128], BF16)
        sq_s = sb("sq_s", [128, 1024], BF16)
        wqk = sb("wqk", [128, 2, 2, NKC, 128], BF16)

        pA = pst("pA", [128, 1024])
        pB = pst("pB", [128, 1024])
        pC = pst("pC", [128, 1024])
        pD = pst("pD", [128, 1024])
        ps_tr = pD[:, 0:512]

        p = Prog(nc, ctx)
        p.add_sem("sx")
        p.add_sem("sx2")
        p.add_sem("sk")
        p.add_sem("skg")
        p.add_sem("swa")
        p.add_sem("swb")
        p.add_sem("sfa")
        p.add_sem("sfb")
        p.add_sem("sd")
        p.add_sem("sx3")
        p.add_sem("swo")
        p.add_sem("st", "tensor")
        p.add_sem("sc", "scalar")
        p.add_sem("sv", "vector")
        p.add_sem("sg", "gpsimd")
        p.add_sem("so")

        xq = x_h.rearrange("(tt p) d -> p tt d", p=128)
        p.emit("sync", lambda e: e.dma_start(out=qb_s[:], in_=qb2_h),
               inc="sk", inc_by=16)
        p.emit("sync", lambda e: e.dma_start(out=kb_s[:], in_=kb2_h),
               inc="sk", inc_by=16)
        p.emit("gpsimd", lambda e: e.dma_start(out=ident_bf[:], in_=ident_h),
               inc="skg", inc_by=16)
        p.emit("gpsimd", lambda e: e.dma_start(out=tri_bf[:], in_=tri_h),
               inc="skg", inc_by=16)
        constg_tok = p.emit("gpsimd", lambda e: e.dma_start(out=obr_bf[:], in_=obr_h),
                            inc="skg", inc_by=16)
        qb_tok = kb_tok = ("sk", 32)
        ident_tok = tri_tok = obr_tok = ("skg", 48)
        ones_tok = p.emit("vector", lambda e: e.memset(ones_bf[:], 1.0), inc="sv")
        trineg_tok = p.emit(
            "gpsimd",
            lambda e: e.tensor_scalar(trineg[:], tri_bf[:], -1.0, 1.0e5,
                                      ALU.add, ALU.mult),
            waits=[tri_tok], inc="sg")
        iota_tok = p.emit("gpsimd",
                          lambda e: e.iota(pvec[:], pattern=[[0, 1]],
                                           channel_multiplier=1, base=-127,
                                           allow_small_or_imprecise_dtypes=True),
                          inc="sg")
        evec_tok = [None] * NH
        for h in range(NH):
            evec_tok[h] = p.emit(
                "scalar",
                lambda e, h=h: e.activation(evec[:, h:h + 1], pvec[:], AF.Exp,
                                            scale=SLOPES[h]),
                waits=[iota_tok], inc="sc")

        ln_state = {"tr_done": [None] * 2, "ev": None}
        exp_rd0_tok = [None]
        exp_wr0_tok = [None]

        def emit_ln_group(tts, xtoks, xaps, rd_out=None):
            t_sum = None
            t_sq = None
            for j, tt in enumerate(tts):
                xt = xaps[j]
                t_sum = p.emit("vector",
                               lambda e, tt=tt, xt=xt: e.reduce_sum(
                                   stats[:, 0, tt:tt + 1], xt, axis=AX.X),
                               waits=[xtoks[j]], inc="sv")
                t_sq = p.emit("scalar",
                              lambda e, tt=tt, xt=xt: e.activation(
                                  sq_s[:], xt, AF.Square,
                                  accum_out=stats[:, 1, tt:tt + 1]),
                              waits=[xtoks[j]], inc="sc")
            g0 = tts[0]
            sl = slice(g0, g0 + len(tts))
            m1 = p.emit("vector",
                        lambda e: e.tensor_scalar(stats[:, 2, sl], stats[:, 0, sl],
                                                  1.0 / D, None, ALU.mult),
                        waits=[t_sum], inc="sv")
            m2 = p.emit("vector",
                        lambda e: e.tensor_tensor(stats[:, 3, sl], stats[:, 2, sl],
                                                  stats[:, 2, sl], ALU.mult),
                        waits=[m1], inc="sv")
            m3 = p.emit("vector",
                        lambda e: e.tensor_scalar(stats[:, 4, sl], stats[:, 1, sl],
                                                  1.0 / D, EPS, ALU.mult, ALU.add),
                        waits=[t_sq], inc="sv")
            m4 = p.emit("vector",
                        lambda e: e.tensor_tensor(stats[:, 4, sl], stats[:, 4, sl],
                                                  stats[:, 3, sl], ALU.subtract),
                        waits=[m2, m3], inc="sv")
            m5 = p.emit("scalar",
                        lambda e: e.activation(stats[:, 3, sl], stats[:, 4, sl],
                                               AF.Sqrt),
                        waits=[m4], inc="sc")
            m6 = p.emit("vector",
                        lambda e: e.reciprocal(stats[:, 4, sl], stats[:, 3, sl]),
                        waits=[m5], inc="sv")
            ev_toks = []
            for j, tt in enumerate(tts):
                xt = xaps[j]
                sl2 = j % 2
                t_xn = p.emit("vector",
                              lambda e, sl2=sl2, tt=tt, xt=xt: e.tensor_scalar(
                                  xn_tm[:, sl2, :], xt,
                                  stats[:, 2, tt:tt + 1],
                                  stats[:, 4, tt:tt + 1],
                                  ALU.subtract, ALU.mult),
                              waits=[m6, ln_state["tr_done"][sl2]], inc="sv")
                ev_tok = None
                for half in range(2):
                    te_tok = None
                    for j4 in range(4):
                        jj = half * 4 + j4
                        te_tok = p.emit(
                            "tensor",
                            lambda e, sl2=sl2, jj=jj, j4=j4: e.matmul(
                                ps_tr[:, j4 * 128:j4 * 128 + 128],
                                xn_tm[:, sl2, jj * 128:jj * 128 + 128],
                                ident_bf[:], start=True, stop=True),
                            waits=[t_xn, ident_tok, ln_state["ev"]], inc="st")
                    ev_tok = p.emit(
                        "scalar",
                        lambda e, half=half, tt=tt: e.activation(
                            xn_fm[:, half * 4:half * 4 + 4, tt * 128:tt * 128 + 128],
                            ps_tr[:].rearrange("p (c h) -> p c h", c=4),
                            AF.Identity),
                        waits=[te_tok], inc="sc")
                    ln_state["ev"] = ev_tok
                ln_state["tr_done"][sl2] = te_tok
                ev_toks.append(ev_tok)
                if rd_out is not None:
                    rd_out.append((t_xn, t_sq))
            return ev_toks

        xn1_tok = [None] * NT
        vready = [None] * NT
        _va_cm = nc.sbuf_tensor("v_aug", [128, NT, NH * 65], BF16)
        v_aug = _va_cm.__enter__()
        if True:
            vones_tok = p.emit(
                "vector",
                lambda e: e.memset(
                    v_aug[:].rearrange("p t (h c) -> p t h c", c=65)
                    [:, :, :, 64:65], 1.0),
                inc="sv")

            def wtile(m):
                return m.rearrange("(kc p) d -> p kc d", p=128)

            _k_cm = nc.sbuf_tensor("k_all", [128, NKC, S], BF16)
            k_all = _k_cm.__enter__()
            _qh_cm = nc.sbuf_tensor("q_hiT", [128, NKC, S // 2], BF16)
            q_hiT = _qh_cm.__enter__()
            _ql_cm = nc.sbuf_tensor("q_loT", [128, NKC, S // 2], BF16)
            q_loT = _ql_cm.__enter__()
            with nc.sbuf_tensor("x_r", [128, 6, D], F32) as x_r, \
                 nc.sbuf_tensor("wv_bf", [128, NKC, D], BF16) as wv_bf, \
                 nc.sbuf_tensor("vbr_bf", [1, D], BF16) as vbr_bf:
                vbr_tok = p.emit("gpsimd",
                                 lambda e: e.dma_start(out=vbr_bf[:], in_=vbr_h),
                                 inc="skg", inc_by=16)
                x_tok = [None] * 8
                xn_rd = {}

                def emit_x_pair(j):
                    sem = "sx" if j % 2 == 0 else "sx2"
                    guards = []
                    for t2 in (2 * j - 6, 2 * j - 5):
                        if t2 >= 0:
                            guards += list(xn_rd[t2])
                    x_tok[j] = p.emit(
                        "sync",
                        lambda e, j=j: e.dma_start(
                            out=x_r[:, (2 * j) % 6:(2 * j) % 6 + 2, :],
                            in_=xq[:, 2 * j:2 * j + 2, :]),
                        waits=guards + [(sem, p.cnt[sem])],
                        inc=sem, inc_by=16)

                for j in range(3):
                    emit_x_pair(j)
                wv_tok = p.emit("gpsimd",
                                lambda e: e.dma_start(out=wv_bf[:], in_=wtile(wvT_h)),
                                waits=[("swa", p.cnt["swa"])],
                                inc="swa", inc_by=16)
                qk_w_tok = {}
                q_tok_d = {}
                k_tok_d = {}
                qk_last = {0: None, 1: None}
                qk_rd = {0: None, 1: None}
                wseq = [0]

                def emit_qk_slices(dc, wait):
                    seq = wseq[0]
                    wseq[0] += 1
                    buf = seq % 2
                    sem = "swa" if buf == 0 else "swb"
                    p.emit("gpsimd",
                           lambda e, dc=dc, buf=buf: e.dma_start(
                               out=wqk[:, buf, 0, :, :],
                               in_=wtile(wqT_h[:, dc * 128:dc * 128 + 128])),
                           waits=[wait, (sem, p.cnt[sem])], inc=sem, inc_by=16)
                    qk_w_tok[seq] = p.emit(
                        "gpsimd",
                        lambda e, dc=dc, buf=buf: e.dma_start(
                            out=wqk[:, buf, 1, :, :],
                            in_=wtile(wkT_h[:, dc * 128:dc * 128 + 128])),
                        inc=sem, inc_by=16)
                    return seq

                prev_ev = {0: None, 1: None}

                def emit_qk_half(tbh, dc):
                    seq = qk_seq[(tbh, dc)]
                    buf = seq % 2
                    for which, dst, bias_sb in ((0, None, qb_s),
                                                (1, k_all, kb_s)):
                        psq = pA if which == 0 else pB
                        mm = None
                        for kc in range(NKC):
                            for tb2 in range(2):
                                tb = tbh * 2 + tb2
                                mm = p.emit(
                                    "tensor",
                                    lambda e, kc=kc, tb=tb, tb2=tb2, psq=psq,
                                    buf=buf, which=which: e.matmul(
                                        psq[:, tb2 * 512:tb2 * 512 + 512],
                                        wqk[:, buf, which, kc, :],
                                        xn_fm[:, kc, tb * 512:tb * 512 + 512],
                                        start=(kc == 0), stop=(kc == NKC - 1)),
                                    waits=[qk_w_tok[seq],
                                           xn1_tok[tbh * 8 + 7],
                                           prev_ev[which]],
                                    inc="st")
                        if which == 0:
                            dstq = q_loT if tbh == 0 else q_hiT
                        else:
                            dstq = None
                        ev = p.emit(
                            "scalar",
                            lambda e, dc=dc, psq=psq, tbh=tbh, dstq=dstq,
                            dst=dst, which=which, bias_sb=bias_sb:
                            e.activation(
                                (dstq[:, dc, :] if which == 0 else
                                 dst[:, dc, tbh * 1024:tbh * 1024 + 1024]),
                                psq[:],
                                AF.Identity, bias=bias_sb[:, dc:dc + 1]),
                            waits=[mm, qb_tok], inc="sc")
                        prev_ev[which] = ev
                        qk_last[buf] = mm
                        qk_rd[buf] = mm
                        if which == 0:
                            q_tok_d[dc] = ev
                        else:
                            k_tok_d[dc] = ev

                qk_seq = {}
                qk_sched = [(0, d) for d in range(NKC)] + \
                           [(1, d) for d in range(NKC)]
                qk_seq[qk_sched[0]] = emit_qk_slices(qk_sched[0][1], None)
                qk_seq[qk_sched[1]] = emit_qk_slices(qk_sched[1][1], None)
                nxt = [2]

                v_ev = [None]
                for g4 in range(4):
                    rd_out = []
                    evs = emit_ln_group(list(range(4 * g4, 4 * g4 + 4)),
                                        [x_tok[(4 * g4 + k2) // 2]
                                         for k2 in range(4)],
                                        [x_r[:, tt % 6, :]
                                         for tt in range(4 * g4, 4 * g4 + 4)],
                                        rd_out)
                    for k2 in range(4):
                        xn_rd[4 * g4 + k2] = rd_out[k2]
                    for j in (3 + 2 * g4, 4 + 2 * g4):
                        if j < 8:
                            emit_x_pair(j)
                    xn1_tok[4 * g4:4 * g4 + 4] = evs
                    for tt in range(4 * g4, 4 * g4 + 4):
                        p.emit("tensor",
                               lambda e: e.matmul(pC[:, 0:512], ones_bf[:],
                                                  vbr_bf[:, 0:512],
                                                  start=True, stop=False),
                               waits=[ones_tok, vbr_tok] +
                               (list(v_ev[0]) if v_ev[0] else []), inc="st")
                        p.emit("tensor",
                               lambda e: e.matmul(pC[:, 512:1024],
                                                  ones_bf[:],
                                                  vbr_bf[:, 512:1024],
                                                  start=True, stop=False),
                               inc="st")
                        tok = None
                        for kc in range(NKC):
                            stp = kc == NKC - 1
                            p.emit("tensor",
                                   lambda e, kc=kc, stp=stp, tt=tt: e.matmul(
                                       pC[:, 0:512],
                                       xn_fm[:, kc, tt * 128:tt * 128 + 128],
                                       wv_bf[:, kc, 0:512], start=False, stop=stp),
                                   waits=[xn1_tok[tt], wv_tok], inc="st")
                            tok = p.emit("tensor",
                                         lambda e, kc=kc, stp=stp, tt=tt:
                                         e.matmul(
                                             pC[:, 512:1024],
                                             xn_fm[:, kc, tt * 128:tt * 128 + 128],
                                             wv_bf[:, kc, 512:1024],
                                             start=False, stop=stp),
                                         inc="st")
                        ev1 = p.emit(
                            "vector",
                            lambda e, tt=tt: e.tensor_copy(
                                v_aug[:, tt, 0:8 * 65].rearrange(
                                    "p (h c) -> p h c", c=65)[:, :, 0:64],
                                pC[:, 0:512].rearrange(
                                    "p (h c) -> p h c", c=64)),
                            waits=[tok, vones_tok], inc="sv")
                        ev2 = p.emit(
                            "scalar",
                            lambda e, tt=tt: e.activation(
                                v_aug[:, tt, 8 * 65:16 * 65].rearrange(
                                    "p (h c) -> p h c", c=65)[:, :, 0:64],
                                pC[:, 512:1024].rearrange(
                                    "p (h c) -> p h c", c=64), AF.Identity),
                            waits=[tok, vones_tok], inc="sc")
                        v_ev[0] = (ev1, ev2)
                        vready[tt] = (ev1, ev2)
                    if g4 >= 1:
                        for _ in range(4 if g4 < 3 else 4):
                            if qk_sched:
                                key = qk_sched.pop(0)
                                emit_qk_half(*key)
                                if nxt[0] < 2 * NKC:
                                    nk = ((0, nxt[0]) if nxt[0] < NKC
                                          else (1, nxt[0] - NKC))
                                    qk_seq[nk] = emit_qk_slices(
                                        nk[1], qk_rd[nxt[0] % 2])
                                    nxt[0] += 1
                v_done, v_done2 = vready[NT - 1]
                while qk_sched:
                    key = qk_sched.pop(0)
                    emit_qk_half(*key)
                    if nxt[0] < 2 * NKC:
                        nk = ((0, nxt[0]) if nxt[0] < NKC
                              else (1, nxt[0] - NKC))
                        qk_seq[nk] = emit_qk_slices(nk[1], qk_rd[nxt[0] % 2])
                        nxt[0] += 1
            if stop < 2:
                return _finish(nc, p)

            ph2_sc = (prev_ev[0], prev_ev[1])
            wo_wqk = wqk[:].rearrange("p a b k c -> p (a b k c)").rearrange(
                "p (k c) -> p k c", c=512)
            wo_wqk_tok = p.emit(
                "gpsimd",
                lambda e: e.dma_start(out=wo_wqk,
                                      in_=wtile(woT_h[:, 0:512])),
                waits=[qk_last[0], qk_last[1]], inc="swo", inc_by=16)

            if stop < 3:
                return _finish(nc, p)

            def group_slot(h, idx, q_lo):
                if h % 2 == 0 and WCH.get(h, 512) == 64:
                    if q_lo % 128 == 64:
                        return pD, 512 + (idx // 2) * 65
                    return pC, (idx // 2) * 65
                return pC, (h % 2) * 512 + idx * 65

            exp_rd = {0: None, 1: None}
            exp_wr = {0: None, 1: None}
            sc_rd = {0: prev_ev[0], 1: prev_ev[1]}
            vs_rd = [None] * 12
            zrec_rd = [None] * 16
            vs_i = [0]
            norm_prev = {}
            pair_norm_last = [None] * NKC
            pending_norms = []

            def flush_norms():
                for fn in pending_norms:
                    fn()
                pending_norms.clear()
            last_scores_tok = [None]
            gstep = [0]

            def q_ref(rows, dc, c0, c1):
                if c1 <= 1024:
                    return q_loT[rows, dc, c0:c1]
                return q_hiT[rows, dc, c0 - 1024:c1 - 1024]

            def ap_ref(ttile):
                if ttile < 8:
                    return ap_lo, ttile
                return ap_hi, ttile - 8

            def attn_block(qb, dc, single_slot, feed):
                q0 = qb * QB
                hA, hB = 2 * dc, 2 * dc + 1
                q_tok = q_tok_d[dc]
                k_tok = k_tok_d[dc]
                groups = []
                for h in (hA, hB):
                    idx = 0
                    for (c0, w, R, klo, khi) in CHUNKS[h]:
                        if c0 >= q0 + QB or c0 + w <= q0:
                            continue
                        if w <= 128:
                            groups.append((h, idx, c0, w, R,
                                           list(range(klo, khi + 1))))
                            idx += 1
                        else:
                            for j in range(c0 // 128, (c0 + w) // 128):
                                groups.append((h, idx, j * 128, 128, R,
                                               list(range(klo, j + 1))))
                                idx += 1
                all_kis = sorted({k for g in groups for k in g[5]})
                act_k = {hA: set(), hB: set()}
                win = {}
                for (h, idx, q_lo_, M, R, kis) in groups:
                    act_k[h].update(kis)
                    for k2 in kis:
                        a, b = win.get((k2, h), (1 << 30, -1))
                        win[(k2, h)] = (min(a, q_lo_), max(b, q_lo_ + M))

                def diag_key(g, ki):
                    h, idx, q_lo_, M, R, kis = g
                    fo = ki * 128 - q0
                    if not (q0 <= ki * 128 < q0 + QB):
                        return False
                    lo = q_lo_ - q0
                    return lo < fo + 128 and lo + M > fo

                glists = {}
                for ki in all_kis:
                    gl = [g for g in groups if ki in g[5]]
                    gl.sort(key=lambda g: diag_key(g, ki))
                    glists[ki] = gl
                bank_first = {}
                bank_last = {}
                for ki in all_kis:
                    for (h, idx, q_lo_, M, R, kis) in glists[ki]:
                        pso, col = group_slot(h, idx, q_lo_)
                        bank = (id(pso), col // 512, q_lo_ % 128)
                        key = (ki, h, idx)
                        bank_first.setdefault(bank, key)
                        bank_last[bank] = key
                vs_map = {}
                ki_av_tok = {}
                etoks = {}
                mtoks = {}

                def emit_scores(i, ki):
                    par = 0 if single_slot else gstep[0] % 2
                    pss = pA if par == 0 else pB
                    eslot = gstep[0] % 2
                    for (h, idx, q_lo_, M, R, kis) in groups:
                        if ki not in kis:
                            continue
                        vkey = (h, R, ki)
                        if vkey in vs_map:
                            continue
                        s2 = math.exp(SLOPES[h] * (ki * 128 + 127 - R))
                        slot = vs_i[0] % 12
                        vs_i[0] += 1
                        veng, vinc = (("gpsimd", "sg") if slot % 2 == 0
                                      else ("vector", "sv"))
                        vtok = p.emit(
                            veng,
                            lambda e, h=h, ki=ki, s2=s2, slot=slot:
                            e.tensor_scalar(
                                vs_s[:, slot, :],
                                v_aug[:, ki, h * 65:h * 65 + 65],
                                evec[:, h:h + 1], float(s2),
                                ALU.mult, ALU.mult),
                            waits=[vready[ki][0], vready[ki][1],
                                   evec_tok[h], vs_rd[slot]],
                            inc=vinc)
                        vs_map[vkey] = (slot, vtok)
                    a_act = ki in act_k[hA]
                    b_act = ki in act_k[hB]
                    sA = sB = None
                    spans = []
                    if a_act:
                        al, ah = win[(ki, hA)]
                        al = max(al, ki * 128)
                        sA = p.emit(
                            "tensor",
                            lambda e, ki=ki, pss=pss, al=al, ah=ah, q0=q0,
                            dc=dc: e.matmul(
                                pss[:, al - q0:ah - q0],
                                k_all[0:64, dc, ki * 128:ki * 128 + 128],
                                q_ref(slice(0, 64), dc, al, ah),
                                start=True, stop=True, tile_position=(0, 0)),
                            waits=[q_tok, k_tok, sc_rd[par]], inc="st")
                        spans.append([al - q0, ah - q0, sA, hA])
                    if b_act:
                        al, ah = win[(ki, hB)]
                        al = max(al, ki * 128)
                        sB = p.emit(
                            "tensor",
                            lambda e, ki=ki, pss=pss, al=al, ah=ah, q0=q0,
                            dc=dc: e.matmul(
                                pss[:, 512 + al - q0:512 + ah - q0],
                                k_all[64:128, dc, ki * 128:ki * 128 + 128],
                                q_ref(slice(64, 128), dc, al, ah),
                                start=True, stop=True, tile_position=(64, 0)),
                            waits=[q_tok, k_tok, sc_rd[par]], inc="st")
                        spans.append([512 + al - q0, 512 + ah - q0, sB, hB])
                    last_scores_tok[0] = sB if sB is not None else sA
                    if len(spans) == 2 and spans[0][1] == spans[1][0]:
                        spans = [[spans[0][0], spans[1][1], sB, None]]
                    etok = None
                    for (c0_, c1_, stok, eh) in spans:
                        etok = p.emit(
                            "scalar",
                            lambda e, pss=pss, eslot=eslot, c0_=c0_, c1_=c1_:
                            e.activation(
                                exp_s[:, eslot, c0_:c1_],
                                pss[:, c0_:c1_], AF.Exp, scale=0.125),
                            waits=[stok, exp_rd[eslot]], inc="sc")
                        if eh is None:
                            etoks[(ki, hA)] = etoks[(ki, hB)] = etok
                        else:
                            etoks[(ki, eh)] = etok
                    sc_rd[par] = etok
                    exp_wr[eslot] = etok
                    if eslot == 0:
                        exp_wr0_tok[0] = etok
                    etoks[ki] = etok
                    if q0 <= ki * 128 < q0 + QB:
                        fo = ki * 128 - q0
                        mtok = None
                        for hh in ([0] if not b_act else
                                   ([1] if not a_act else [0, 1])):
                            mtok = p.emit(
                                "vector",
                                lambda e, eslot=eslot, fo=fo, hh=hh:
                                e.tensor_tensor(
                                    exp_s[:, eslot,
                                          hh * 512 + fo:hh * 512 + fo + 128],
                                    exp_s[:, eslot,
                                          hh * 512 + fo:hh * 512 + fo + 128],
                                    tri_bf[:], ALU.mult),
                                waits=[etoks[(ki, hA if hh == 0 else hB)],
                                       tri_tok], inc="sv")
                        mtoks[ki] = mtok

                def emit_av(i, ki, eslot):
                    mtok = mtoks.get(ki)
                    av_tok = None

                    def is_diag(g):
                        return mtok is not None and diag_key(g, ki)

                    for g in glists[ki]:
                        h, idx, q_lo_, M, R, kis = g
                        etok = etoks[(ki, h)]
                        slot, vtok = vs_map[(h, R, ki)]
                        pso, col = group_slot(h, idx, q_lo_)
                        bank = (id(pso), col // 512, q_lo_ % 128)
                        is_first = bank_first[bank] == (ki, h, idx)
                        is_last = bank_last[bank] == (ki, h, idx)
                        rowb = q_lo_ % 128
                        hcol = (h % 2) * 512 + (q_lo_ - q0)
                        tp = (0, 64) if rowb == 64 else None
                        av_tok = p.emit(
                            "tensor",
                            lambda e, eslot=eslot, hcol=hcol, M=M, pso=pso,
                            col=col, rowb=rowb, slot=slot, f=is_first,
                            l=is_last, tp=tp: e.matmul(
                                pso[rowb:rowb + M, col:col + 65],
                                exp_s[:, eslot, hcol:hcol + M],
                                vs_s[:, slot, :],
                                start=f, stop=l, tile_position=tp),
                            waits=[etok, mtok if is_diag(g) else None,
                                   vtok,
                                   norm_prev.get(bank, v_done),
                                   None if bank in norm_prev else v_done2],
                            inc="st")
                        vs_rd[slot] = av_tok
                        ki_av_tok[ki] = av_tok
                    if av_tok is not None:
                        exp_rd[eslot] = av_tok
                        if eslot == 0:
                            exp_rd0_tok[0] = av_tok

                prev = None
                for i, ki in enumerate(all_kis):
                    if feed is not None:
                        feed()
                    emit_scores(i, ki)
                    if i == 0:
                        flush_norms()
                    if prev is not None:
                        emit_av(*prev)
                    prev = (i, ki, gstep[0] % 2)
                    gstep[0] += 1
                emit_av(*prev)
                qb_av_done = ki_av_tok[all_kis[-1]]
                pending_norms.append(
                    lambda qb_av_done=qb_av_done, groups=groups, dc=dc:
                    emit_norms(qb_av_done, groups, dc))
                return
            def emit_norms(qb_av_done, groups, dc):
                runs = {}
                for gi, (h, idx, q_lo_, M, R, kis) in enumerate(groups):
                    pso, col = group_slot(h, idx, q_lo_)
                    key = (id(pso), col // 512, q_lo_ % 128)
                    runs.setdefault(key, (pso, []))[1].append((col, gi))
                zslot_of = {}
                rtok_of = {}
                zi = 0
                for (kid, chalf, rowb), (pso, cols) in runs.items():
                    cols.sort()
                    c0 = cols[0][0]
                    n = len(cols)
                    M = 64 if rowb == 64 else (
                        64 if WCH.get(groups[cols[0][1]][0], 512) == 64 else 128)
                    zs = [None] * n
                    for j, (c, gi) in enumerate(cols):
                        zs[j] = zi % 16
                        zslot_of[gi] = zi % 16
                        zi += 1
                    assert zs == list(range(zs[0], zs[0] + n)), zs
                    rt = p.emit(
                        "vector",
                        lambda e, pso=pso, c0=c0, n=n, rowb=rowb, M=M, z0=zs[0]:
                        e.reciprocal(
                            zrec[rowb:rowb + M, z0:z0 + n],
                            pso[rowb:rowb + M, c0:c0 + n * 65].rearrange(
                                "p (n c) -> p n c", c=65)[:, :, 64:65]
                            .rearrange("p n c -> p (n c)")
                            if n > 1 else
                            pso[rowb:rowb + M, c0 + 64:c0 + 65]),
                        waits=[qb_av_done] + [zrec_rd[z] for z in zs],
                        inc="sv")
                    for (c, gi) in cols:
                        rtok_of[gi] = rt
                for gi, (h, idx, q_lo_, M, R, kis) in enumerate(groups):
                    pso, col = group_slot(h, idx, q_lo_)
                    bank = (id(pso), col // 512, q_lo_ % 128)
                    rowb = q_lo_ % 128
                    zslot = zslot_of[gi]
                    rtok = rtok_of[gi]
                    apb, att = ap_ref(q_lo_ // 128)
                    ntok = p.emit(
                        "vector",
                        lambda e, pso=pso, col=col, rowb=rowb, M=M, zslot=zslot,
                        h=h, dc=dc, apb=apb, att=att: e.tensor_scalar(
                            apb[rowb:rowb + M, att,
                                dc * 128 + (h % 2) * 64:
                                dc * 128 + (h % 2) * 64 + 64],
                            pso[rowb:rowb + M, col:col + 64],
                            zrec[rowb:rowb + M, zslot:zslot + 1], None,
                            ALU.mult),
                        waits=[rtok], inc="sv")
                    zrec_rd[zslot] = ntok
                    norm_prev[bank] = ntok
                    pair_norm_last[dc] = ntok

            _aph_cm = nc.sbuf_tensor("ap_hi", [128, 8, D], BF16, side="right")
            ap_hi = _aph_cm.__enter__()
            _apl_cm = nc.sbuf_tensor("ap_lo", [128, 8, D], BF16, side="right")
            ap_lo = _apl_cm.__enter__()
            hold = {}

            for qb in (0, 1):
                for dc in range(NKC):
                    attn_block(qb, dc, False, None)
            flush_norms()
            pnl01 = list(pair_norm_last)
            pb_guard = sc_rd[1]
            q_lo_last = last_scores_tok[0]
            _ql_cm.__exit__(None, None, None)
            if stop < 4:
                return _finish(nc, p)

            wo_toks = {}
            wo_rd = [None]

            def emit_wo_load(half, wait):
                wo_h = hold["wo_h"]
                wo_toks[half] = p.emit(
                    "gpsimd",
                    lambda e, half=half, wo_h=wo_h: e.dma_start(
                        out=wo_h[:],
                        in_=wtile(woT_h[:, half * 512:half * 512 + 512])),
                    waits=[wait, ("swo", p.cnt["swo"])], inc="swo", inc_by=16)

            radd_tok = [None] * NT
            xn2_tok = [None] * NT

            def gen_th0():
                for dcc in range(NKC):
                    for half in range(2):
                        te_tok = None
                        for j4 in range(4):
                            tt = half * 4 + j4
                            te_tok = p.emit(
                                "tensor",
                                lambda e, tt=tt, j4=j4, dcc=dcc: e.matmul(
                                    pB[:, j4 * 128:j4 * 128 + 128],
                                    ap_lo[:, tt, dcc * 128:dcc * 128 + 128],
                                    ident_bf[:], start=True, stop=True),
                                waits=[pnl01[dcc], ident_tok, pb_guard,
                                       ln_state["ev"]], inc="st")
                            yield
                        ev = p.emit(
                            "vector",
                            lambda e, dcc=dcc, half=half: e.tensor_copy(
                                xn_fm[:, dcc, half * 512:half * 512 + 512],
                                pB[:, 0:512]),
                            waits=[te_tok], inc="sv")
                        ln_state["ev"] = ev
                        yield
                tr0_done = ln_state["ev"]
                _apl_cm.__exit__(None, None, None)
                _wo_cm = nc.sbuf_tensor("wo_h", [128, NKC, 512], BF16,
                                        side="right")
                hold["wo_h"] = _wo_cm.__enter__()
                _x2h0_cm = nc.sbuf_tensor("x2h0", [128, 8, D], F32,
                                          side="right")
                x2h0 = _x2h0_cm.__enter__()
                hold["x2h0"] = x2h0
                hold["x2h0_cm"] = _x2h0_cm
                x2h0_tok1 = p.emit("sync",
                                   lambda e: e.dma_start(out=x2h0[:, 0:6, :],
                                                         in_=xq[:, 0:6, :]),
                                   waits=[q_lo_last, v_done, v_done2, ph2_sc[0],
                                          ph2_sc[1]],
                                   inc="sx3", inc_by=16)
                x2h0_tok = p.emit("sync",
                                  lambda e: e.dma_start(out=x2h0[:, 6:8, :],
                                                        in_=xq[:, 6:8, :]),
                                  waits=[tr0_done, ("sx3", p.cnt["sx3"])],
                                  inc="sx3", inc_by=16)
                emit_wo_load(1, tr0_done)
                y_ev = {0: None, 1: None}
                for half in range(2):
                    wsrc = wo_wqk if half == 0 else hold["wo_h"][:]
                    wtok = wo_wqk_tok if half == 0 else wo_toks[1]
                    for tt in range(8):
                        psy = pB[:, 0:512] if tt % 2 == 0 else pB[:, 512:1024]
                        p.emit("tensor",
                               lambda e, psy=psy, half=half, wtok=wtok:
                               e.matmul(
                                   psy, ones_bf[:],
                                   obr_bf[:, half * 512:half * 512 + 512],
                                   start=True, stop=False),
                               waits=[obr_tok, tr0_done, y_ev[tt % 2],
                                      wtok], inc="st")
                        mm = None
                        for kc in range(NKC):
                            mm = p.emit(
                                "tensor",
                                lambda e, kc=kc, tt=tt, psy=psy,
                                wsrc=wsrc: e.matmul(
                                    psy,
                                    xn_fm[:, kc, tt * 128:tt * 128 + 128],
                                    wsrc[:, kc, :],
                                    start=False, stop=(kc == NKC - 1)),
                                inc="st")
                            yield
                        wo_rd[0] = mm
                        radd = p.emit(
                            "vector",
                            lambda e, tt=tt, psy=psy, half=half:
                            e.tensor_tensor(
                                x2h0[:, tt, half * 512:half * 512 + 512],
                                x2h0[:, tt, half * 512:half * 512 + 512],
                                psy, ALU.add),
                            waits=[mm, x2h0_tok1 if tt < 6 else x2h0_tok],
                            inc="sv")
                        y_ev[tt % 2] = radd
                        if half == 1:
                            radd_tok[tt] = radd
                        yield
                    if half == 0:
                        hold["wo_rd_h0"] = wo_rd[0]
                for g4o in range(2):
                    evs = emit_ln_group(list(range(4 * g4o, 4 * g4o + 4)),
                                        radd_tok[4 * g4o:4 * g4o + 4],
                                        [x2h0[:, tt, :]
                                         for tt in range(4 * g4o, 4 * g4o + 4)])
                    xn2_tok[4 * g4o:4 * g4o + 4] = evs
                    yield

            gen = gen_th0()
            fed = [0]

            def feed():
                for _ in range(3):
                    if next(gen, "END") == "END":
                        break

            order = [(2, 0), (3, 0)] + [(2, d) for d in range(1, NKC)] + \
                    [(3, d) for d in range(1, NKC)]
            for qb, dc in order:
                attn_block(qb, dc, False, None)
            flush_norms()
            pb_guard = sc_rd[1]
            for _ in gen:
                pass
            pnl = list(pair_norm_last)

            x2h0 = hold["x2h0"]
            ve_snap = ("sv", p.cnt["sv"])
            pa_guard = sc_rd[0]
            qk_post = last_scores_tok[0]
            _qh_cm.__exit__(None, None, None)
            _k_cm.__exit__(None, None, None)
            _va_cm.__exit__(None, None, None)
            if stop < 5:
                return _finish(nc, p)

            with nc.sbuf_tensor("h_fm", [128, NHC, S // 2], BF16) as h_fm, \
                 nc.sbuf_tensor("wd_bf", [128, 2, NHC, 256], BF16) as wd_bf:
                wgu = wqk
                wgT_q = wgT_h.rearrange("(kc p) hh -> p kc hh", p=128)
                wuT_q = wuT_h.rearrange("(kc p) hh -> p kc hh", p=128)
                gu_tok = {}

                def emit_gu_load(m, wait):
                    hc = m % NHC
                    buf = m % 2
                    sem = "sfa" if buf == 0 else "sfb"
                    p.emit("gpsimd",
                           lambda e, hc=hc, buf=buf: e.dma_start(
                               out=wgu[:, buf, 0, :, :],
                               in_=wgT_q[:, :, hc * 128:hc * 128 + 128]),
                           waits=[wait], inc=sem, inc_by=16)
                    gu_tok[m] = p.emit(
                        "gpsimd",
                        lambda e, hc=hc, buf=buf: e.dma_start(
                            out=wgu[:, buf, 1, :, :],
                            in_=wuT_q[:, :, hc * 128:hc * 128 + 128]),
                        inc=sem, inc_by=16)

                emit_gu_load(0, hold["wo_rd_h0"])
                emit_gu_load(1, hold["wo_rd_h0"])
                emit_wo_load(0, wo_rd[0])
                wdT_q = wdT_h.rearrange("(hc p) d -> p hc d", p=128)
                down_ev = {}
                sil_prev = {0: None, 1: None}
                mul_prev = {0: None, 1: None}
                fst = {"wd_rd": None, "down_done": None, "h_done": None,
                       "last_store": None}

                def emit_ffn_half(th):
                    tbase = th * 1024
                    g_ev = None
                    for hc in range(NHC):
                        m = th * NHC + hc
                        buf = m % 2
                        par = hc % 2
                        pg = pA if par == 0 else pB
                        pu = pC if par == 0 else pD
                        mmg = None
                        for kc in range(NKC):
                            for tb2 in range(2):
                                mmg = p.emit(
                                    "tensor",
                                    lambda e, kc=kc, tb2=tb2, buf=buf,
                                    tbase=tbase, pg=pg: e.matmul(
                                        pg[:, tb2 * 512:tb2 * 512 + 512],
                                        wgu[:, buf, 0, kc, :],
                                        xn_fm[:, kc, tbase + tb2 * 512:
                                              tbase + tb2 * 512 + 512],
                                        start=(kc == 0), stop=(kc == NKC - 1)),
                                    waits=[gu_tok[m], xn2_tok[th * 8 + 7],
                                           sil_prev[par], fst["down_done"],
                                           pa_guard, ve_snap], inc="st")
                        mmu = None
                        for kc in range(NKC):
                            for tb2 in range(2):
                                mmu = p.emit(
                                    "tensor",
                                    lambda e, kc=kc, tb2=tb2, buf=buf,
                                    tbase=tbase, pu=pu: e.matmul(
                                        pu[:, tb2 * 512:tb2 * 512 + 512],
                                        wgu[:, buf, 1, kc, :],
                                        xn_fm[:, kc, tbase + tb2 * 512:
                                              tbase + tb2 * 512 + 512],
                                        start=(kc == 0), stop=(kc == NKC - 1)),
                                    waits=[mul_prev[par], ve_snap,
                                           xn2_tok[th * 8 + 7] if par == 1
                                           else None], inc="st")
                        s1 = p.emit("scalar",
                                    lambda e, par=par, pg=pg: e.activation(
                                        exp_s[:, par, :],
                                        pg[:], AF.Silu, scale=QSCALE),
                                    waits=[mmg, exp_rd[0], exp_rd[1],
                                           mul_prev[par]], inc="sc")
                        sil_prev[par] = s1
                        g_ev = p.emit(
                            "vector",
                            lambda e, hc=hc, par=par, pu=pu:
                            e.scalar_tensor_tensor(
                                h_fm[:, hc, :], pu[:], QSCALE,
                                exp_s[:, par, :], ALU.mult, ALU.mult),
                            waits=[mmu, s1, qk_post], inc="sv")
                        mul_prev[par] = g_ev
                        if m + 2 < 2 * NHC:
                            emit_gu_load(m + 2, mmu)
                    return g_ev

                wd_tok = {}
                wd_rdb = {0: None, 1: None}

                def emit_wd_load(q, wait):
                    buf = q % 2
                    wd_tok[q] = p.emit(
                        "gpsimd",
                        lambda e, q=q, buf=buf: e.dma_start(
                            out=wd_bf[:, buf, :, :],
                            in_=wdT_q[:, :, q * 256:q * 256 + 256]),
                        waits=[wait, qk_post, ("sd", p.cnt["sd"])],
                        inc="sd", inc_by=16)

                def emit_down(th, xbuf):
                    for q in range(4):
                        buf = q % 2
                        mm = None
                        for tt8 in range(8):
                            tt = th * 8 + tt8
                            rkey = (tt8 % 2, q % 2)
                            psd = pB[:, (tt8 % 2) * 512 + (q % 2) * 256:
                                     (tt8 % 2) * 512 + (q % 2) * 256 + 256]
                            mm = None
                            for hc in range(NHC):
                                mm = p.emit(
                                    "tensor",
                                    lambda e, hc=hc, tt8=tt8, psd=psd, buf=buf:
                                    e.matmul(
                                        psd,
                                        h_fm[:, hc, tt8 * 128:tt8 * 128 + 128],
                                        wd_bf[:, buf, hc, :],
                                        start=(hc == 0), stop=(hc == NHC - 1)),
                                    waits=[wd_tok[q], fst["h_done"],
                                           sil_prev[1], down_ev.get(rkey)],
                                    inc="st")
                            ev = p.emit(
                                "vector",
                                lambda e, tt8=tt8, q=q, psd=psd,
                                xbuf=xbuf: e.scalar_tensor_tensor(
                                    xbuf[:, tt8, q * 256:q * 256 + 256],
                                    psd, QSCALE,
                                    xbuf[:, tt8, q * 256:q * 256 + 256],
                                    ALU.mult, ALU.add),
                                waits=[mm], inc="sv")
                            down_ev[rkey] = ev
                            fst["down_done"] = ev
                            fst["last_store"] = p.emit(
                                "sync",
                                lambda e, tt=tt, tt8=tt8, q=q, xbuf=xbuf:
                                e.dma_start(
                                    out=out_h[tt * 128:tt * 128 + 128,
                                              q * 256:q * 256 + 256],
                                    in_=xbuf[:, tt8, q * 256:q * 256 + 256]),
                                waits=[ev], inc="so", inc_by=16)
                        wd_rdb[buf] = mm
                        if q + 2 < 4:
                            emit_wd_load(q + 2, mm)

                emit_wd_load(0, wd_rdb[0])
                emit_wd_load(1, wd_rdb[1])
                fst["h_done"] = emit_ffn_half(0)
                emit_down(0, x2h0)
                if stop < 7:
                    p.final_wait("sync", fst["last_store"])
                    return _finish(nc, p)
                hold["x2h0_cm"].__exit__(None, None, None)

                evg = fst["down_done"]
                for dcc in range(NKC):
                    for half in range(2):
                        te_tok = None
                        for j4 in range(4):
                            lt = half * 4 + j4
                            te_tok = p.emit(
                                "tensor",
                                lambda e, lt=lt, j4=j4, dcc=dcc: e.matmul(
                                    pB[:, j4 * 128:j4 * 128 + 128],
                                    ap_hi[:, lt, dcc * 128:dcc * 128 + 128],
                                    ident_bf[:], start=True, stop=True),
                                waits=[pnl[dcc], ident_tok, evg,
                                       sil_prev[1]], inc="st")
                        evg = p.emit(
                            "vector",
                            lambda e, dcc=dcc, half=half: e.tensor_copy(
                                xn_fm[:, dcc, 1024 + half * 512:
                                      1024 + half * 512 + 512],
                                pB[:, 0:512]),
                            waits=[te_tok], inc="sv")
                tr1_done = evg

                _x2h1_cm = nc.sbuf_tensor("x2h1", [128, 8, D], F32)
                x2h1 = _x2h1_cm.__enter__()
                x2h1_tok = p.emit(
                    "sync",
                    lambda e: e.dma_start(out=x2h1[:], in_=xq[:, 8:16, :]),
                    waits=[v_done, v_done2, fst["down_done"], sil_prev[1], qk_post,
                           fst["last_store"]],
                    inc="sx3", inc_by=16)
                y_ev = {0: None, 1: None}
                for half in range(2):
                    if half == 1:
                        emit_wo_load(1, wo_rd[0])
                    for tt8 in range(8):
                        tt = 8 + tt8
                        psy = pB[:, 0:512] if tt8 % 2 == 0 else pB[:, 512:1024]
                        p.emit("tensor",
                               lambda e, psy=psy, half=half: e.matmul(
                                   psy, ones_bf[:],
                                   obr_bf[:, half * 512:half * 512 + 512],
                                   start=True, stop=False),
                               waits=[obr_tok, tr1_done, y_ev[tt8 % 2],
                                      wo_toks[half]], inc="st")
                        mm = None
                        for kc in range(NKC):
                            mm = p.emit(
                                "tensor",
                                lambda e, kc=kc, tt=tt, psy=psy,
                                wo_h=hold["wo_h"]: e.matmul(
                                    psy,
                                    xn_fm[:, kc, tt * 128:tt * 128 + 128],
                                    wo_h[:, kc, :],
                                    start=False, stop=(kc == NKC - 1)),
                                inc="st")
                        wo_rd[0] = mm
                        radd = p.emit(
                            "vector",
                            lambda e, tt8=tt8, psy=psy, half=half:
                            e.tensor_tensor(
                                x2h1[:, tt8, half * 512:half * 512 + 512],
                                x2h1[:, tt8, half * 512:half * 512 + 512],
                                psy, ALU.add),
                            waits=[mm, x2h1_tok], inc="sv")
                        y_ev[tt8 % 2] = radd
                        if half == 1:
                            radd_tok[tt] = radd
                ln_state["ev"] = mul_prev[1]
                for g4o in (2, 3):
                    evs = emit_ln_group(list(range(4 * g4o, 4 * g4o + 4)),
                                        radd_tok[4 * g4o:4 * g4o + 4],
                                        [x2h1[:, tt - 8, :]
                                         for tt in range(4 * g4o, 4 * g4o + 4)])
                    xn2_tok[4 * g4o:4 * g4o + 4] = evs

                if stop >= 8:
                    emit_wd_load(0, wd_rdb[0])
                    emit_wd_load(1, wd_rdb[1])
                    fst["h_done"] = emit_ffn_half(1)
                    emit_down(1, x2h1)
                p.final_wait("sync", fst["last_store"])
                _x2h1_cm.__exit__(None, None, None)

        return _finish(nc, p)


def kernel(L0_attn_x, L0_attn_q_w, L0_attn_q_b, L0_attn_k_w, L0_attn_k_b,
           L0_attn_v_w, L0_attn_v_b, L0_attn_o_w, L0_attn_o_b,
           L0_ffn_w_gate, L0_ffn_w_up, L0_ffn_w_down):
    x = np.ascontiguousarray(np.asarray(L0_attn_x, dtype=np.float32))
    wqT = np.ascontiguousarray(np.asarray(L0_attn_q_w, np.float32).T)
    wkT = np.ascontiguousarray(np.asarray(L0_attn_k_w, np.float32).T)
    wvT = np.ascontiguousarray(np.asarray(L0_attn_v_w, np.float32).T)
    woT = np.ascontiguousarray(np.asarray(L0_attn_o_w, np.float32).T)
    qb2 = np.ascontiguousarray(np.asarray(L0_attn_q_b, np.float32).reshape(NKC, 128).T)
    kb2 = np.ascontiguousarray(np.asarray(L0_attn_k_b, np.float32).reshape(NKC, 128).T)
    vbr = np.ascontiguousarray(np.asarray(L0_attn_v_b, np.float32).reshape(1, D))
    obr = np.ascontiguousarray(np.asarray(L0_attn_o_b, np.float32).reshape(1, D))
    wgT = np.ascontiguousarray(np.asarray(L0_ffn_w_gate, np.int32).T)
    wuT = np.ascontiguousarray(np.asarray(L0_ffn_w_up, np.int32).T)
    wdT = np.ascontiguousarray(np.asarray(L0_ffn_w_down, np.int32).T)
    ident = np.eye(128, dtype=np.float32)
    pp, ff = np.mgrid[0:128, 0:128]
    tri = (pp <= ff).astype(np.float32)

    nc = build_program()
    shared = dict(wqT=wqT, wkT=wkT, wvT=wvT, woT=woT, qb2=qb2, kb2=kb2,
                  vbr=vbr, obr=obr, wgT=wgT, wuT=wuT, wdT=wdT,
                  ident=ident, tri=tri)
    in_maps = [dict(x=np.ascontiguousarray(x[c]), **shared) for c in range(8)]
    res = run_bass_kernel_spmd(nc, in_maps, core_ids=list(range(8)))
    return np.stack([res.results[c]["out"] for c in range(8)], axis=0)
